# version 06
# baseline (speedup 1.0000x reference)
import math
import numpy as np
from contextlib import ExitStack

import concourse.bass as bass
import concourse.mybir as mybir
from concourse.bass_utils import run_bass_kernel_spmd

F32 = mybir.dt.float32
BF16 = mybir.dt.bfloat16
AF = mybir.ActivationFunctionType
ALU = mybir.AluOpType
AX = mybir.AxisListType

T = 8192
D = 2048
HD = 128
NCORES = 8
TOWN = T // NCORES
NTB = T // 128
NTB_OWN = TOWN // 128
SCALE = 1.0 / math.sqrt(HD)
FEXP = 1408
NFC = FEXP // 128
NEXP = 10
NHC = D // 128
EPS = 1e-5
NEG = -1e30

SB0 = 16512
TOP = 229344

def _al(x, a=128):
    return (x + a - 1) // a * a

O_ones_f = SB0
O_ident_f = _al(O_ones_f + 128 * 4)
O_ones_b = _al(O_ident_f + 128 * 4)
O_cos = _al(O_ones_b + 128 * 2)
O_sin = _al(O_cos + 4096 * 4)
O_mask = _al(O_sin + 4096 * 4)
O_small = _al(O_mask + 4 * 512 * 4)
O_gw = _al(O_small + 1024 * 4)
O_identb = _al(O_gw + NHC * 8 * 2)
CONST_END = _al(O_identb + 128 * 2)

O_Q = CONST_END
O_K = _al(O_Q + 2 * T * 2)
O_V = _al(O_K + 2 * T * 2)
PH1_BASE = _al(O_V + NTB * 256 * 2)
O_xt = PH1_BASE
O_xn = _al(O_xt + 2 * D * 4)
O_xnT = _al(O_xn + 3 * D * 2)
O_qkt = _al(O_xnT + 2 * NHC * 128 * 2)
O_sqs = _al(O_qkt + 2 * 512 * 2)
O_wqkv = _al(O_sqs + 512 * 2)
PH1_END = _al(O_wqkv + NHC * 768 * 2)
assert PH1_END <= TOP, PH1_END
O_ao = PH1_BASE
O_P = _al(O_ao + 2 * T * 2)
O_repl = _al(O_P + 2 * 1024 * 2)
O_wo = _al(O_repl + 512 * 4)
O_stage = _al(O_wo + 2 * D * 2)
PH2_END = _al(O_stage + 2 * D * 4)
assert PH2_END <= TOP, PH2_END
O_acc = CONST_END
O_xnTo = _al(O_acc + NTB_OWN * D * 4)
PH3A = _al(O_xnTo + NHC * TOWN * 2)
O_ssb = PH3A
R1 = _al(O_ssb + NFC * TOWN * 2)
O_x3a = R1
O_xn3a = PH1_BASE
R1_END_A = _al(O_x3a + 2 * D * 4)
O_gwb = R1
O_uwb = _al(O_gwb + 2 * NHC * 128 * 2)
O_dwb = _al(O_uwb + 2 * NHC * 128 * 2)
O_silu = _al(O_dwb + 4 * 512 * 2)
O_wbc = _al(O_silu + 2 * 512 * 2)
R1_END_B = _al(O_wbc + TOWN * 2)
R1_END = max(R1_END_A, R1_END_B)
assert R1_END <= TOP, R1_END
O_x3b = _al(R1_END)
O_apj = O_gwb
O_out = O_ssb
O_wmrr = O_x3b
PH4_END = _al(O_x3b + 2 * D * 4)
assert PH4_END <= TOP, PH4_END

def _stats(par):
    b = par * 8
    return dict(s1=b, s2=b + 1, mean=b + 2, var=b + 3, rstd=b + 4, nmr=b + 5)
C_QKMAX = 16
C_NT = 20
C_M2 = 24
C_MVAL = 25
C_BIAS = 26
C_LG = 32
C_SC1 = 40
C_SC2 = 48
C_GS = 56
C_RT = 60
C_RECIP = 64
C_U1 = 576
C_U2 = 640
C_NC = 704
C_U3 = 708
C_U4 = 772


def build_nc():
    PH = _BUILD_PHASES
    ACH = _ATTN_CHUNKS

    nc = bass.Bass("TRN2", target_bir_lowering=False)

    x_full = nc.dram_tensor("x_full", [T, D], F32, kind="ExternalInput").ap()
    x_own = nc.dram_tensor("x_own", [TOWN, D], F32, kind="ExternalInput").ap()
    wqkv_d = nc.dram_tensor("wqkv", [128, NHC * 768], F32, kind="ExternalInput").ap()
    wo_d = nc.dram_tensor("wo", [128, 2 * D], F32, kind="ExternalInput").ap()
    gwt_d = nc.dram_tensor("gwt", [D, 8], F32, kind="ExternalInput").ap()
    gates_d = nc.dram_tensor("gates", [NEXP, NFC, 128, NHC * 128], F32, kind="ExternalInput").ap()
    ups_d = nc.dram_tensor("ups", [NEXP, NFC, 128, NHC * 128], F32, kind="ExternalInput").ap()
    downs_d = nc.dram_tensor("downs", [NEXP, 4, NFC, 128, 512], F32, kind="ExternalInput").ap()
    cos_d = nc.dram_tensor("cos_tok", [128, 4096], F32, kind="ExternalInput").ap()
    sin_d = nc.dram_tensor("sin_tok", [128, 4096], F32, kind="ExternalInput").ap()
    masks_d = nc.dram_tensor("masks", [128, 4 * 512], F32, kind="ExternalInput").ap()
    ones_d = nc.dram_tensor("ones_in", [128, 128], F32, kind="ExternalInput").ap()
    ident_d = nc.dram_tensor("ident_in", [128, 128], F32, kind="ExternalInput").ap()
    y_d = nc.dram_tensor("y", [TOWN, D], F32, kind="ExternalOutput").ap()
    cc_in = nc.dram_tensor("cc_in", [T, D], F32, kind="Internal").ap()
    cc_out = nc.dram_tensor("cc_out", [TOWN, D], F32, kind="Internal").ap()

    def sb(name, shape, dtype, off):
        return nc.alloc_sbuf_tensor_at(name, shape, dtype, offset=off).ap()

    ones_f = sb("ones_f", [128, 128], F32, O_ones_f)
    ident_f = sb("ident_f", [128, 128], F32, O_ident_f)
    ones_b = sb("ones_b", [128, 128], BF16, O_ones_b)
    cos_t = sb("cos_t", [128, 4096], F32, O_cos)
    sin_t = sb("sin_t", [128, 4096], F32, O_sin)
    mask_t = sb("mask_t", [128, 4 * 512], F32, O_mask)
    small = sb("small", [128, 1024], F32, O_small)
    gw_sb = sb("gw_sb", [128, NHC * 8], BF16, O_gw)
    ident_b = sb("ident_b", [128, 128], BF16, O_identb)

    Q_sb = sb("Q_sb", [128, 2 * T], BF16, O_Q)
    K_sb = sb("K_sb", [128, 2 * T], BF16, O_K)
    V_sb = sb("V_sb", [128, NTB * 256], BF16, O_V)
    xt_sb = sb("xt_sb", [128, 2 * D], F32, O_xt)
    xn_sb = sb("xn_sb", [128, 3 * D], BF16, O_xn)
    xnT_sb = sb("xnT_sb", [128, 2 * NHC * 128], BF16, O_xnT)
    qkt_sb = sb("qkt_sb", [128, 2 * 512], BF16, O_qkt)
    sqs_sb = sb("sqs_sb", [128, 512], BF16, O_sqs)
    wqkv_sb = sb("wqkv_sb", [128, NHC * 768], BF16, O_wqkv)

    ao_sb = sb("ao_sb", [128, 2 * T], BF16, O_ao)
    wmr_sb = sb("wmr_sb", [1, 8 * TOWN], BF16, O_wmrr)
    P_sb = sb("P_sb", [128, 2 * 1024], BF16, O_P)
    repl_sb = sb("repl_sb", [128, 512], F32, O_repl)
    wo_sb = sb("wo_sb", [128, 2 * D], BF16, O_wo)
    stage_sb = sb("stage_sb", [128, 2 * D], F32, O_stage)

    acc_sb = sb("acc_sb", [128, NTB_OWN * D], F32, O_acc)
    xnTo_sb = sb("xnTo_sb", [128, NHC * TOWN], BF16, O_xnTo)
    s_sb = sb("s_sb", [128, NFC * TOWN], BF16, O_ssb)
    x3a_sb = sb("x3a_sb", [128, 2 * D], F32, O_x3a)
    xn3a_sb = sb("xn3a_sb", [128, 2 * D], BF16, O_xn3a)
    gwb_sb = sb("gwb_sb", [128, 2 * NHC * 128], BF16, O_gwb)
    uwb_sb = sb("uwb_sb", [128, 2 * NHC * 128], BF16, O_uwb)
    dwb_sb = sb("dwb_sb", [128, 4 * 512], BF16, O_dwb)
    silu_sb = sb("silu_sb", [128, 2 * 512], BF16, O_silu)
    wbc_sb = sb("wbc_sb", [128, TOWN], BF16, O_wbc)
    x3b_sb = sb("x3b_sb", [128, 2 * D], F32, O_x3b)
    apj_sb = sb("apj_sb", [128, 2 * D], F32, O_apj)
    out_sb = sb("out_sb", [128, 2 * D], F32, O_out)

    def smc(c, w=1):
        return small[:, c:c + w]
    recip_row = small[0:1, C_RECIP:C_RECIP + 512]

    pb01 = nc.alloc_psum_tensor("pb01", [128, 1024], F32).ap()
    pb23 = nc.alloc_psum_tensor("pb23", [128, 1024], F32).ap()
    pb_hi = [nc.alloc_psum_tensor(f"pb{i}", [128, 512], F32).ap() for i in range(4, 8)]
    pb = [pb01[:, 0:512], pb01[:, 512:1024], pb23[:, 0:512], pb23[:, 512:1024]] + pb_hi

    st_ = ExitStack()
    sems = {}
    cnt = {}
    for name in ("sx0", "sx1", "swg", "sSq", "sStat", "sXNorm",
                 "sT10", "sT11", "sT20", "sT21", "sT30", "sT31",
                 "sStgD0", "sStgD1", "sX40", "sX41", "sApj0", "sApj1",
                 "sOutD0", "sOutD1", "sGU0", "sGU1", "sDN0", "sDN1", "sDN2", "sDN3", "sTr", "sTrE", "sQT", "sT2e", "sTr3", "sTrE3",
                 "sConst", "sPQK", "sPV",
                 "sRope", "sT2", "sVev", "sNrm1", "sNrm2", "sNrm3", "sNrm4",
                 "sNrm5", "sS", "sM", "sE", "sAVt", "sR", "sRep",
                 "sEvac", "sWo", "sStg", "sStgD", "sCC", "sT3",
                 "sLg", "sLgC", "sWm", "sWmT", "sWmTc", "sRepW", "sWbC",
                 "sG", "sU", "sSil", "sS3", "sDwn", "sAcc", "sApj", "sOut",
                 "sOutD", "sX3", "sSq3", "sStat3", "sXN3", "sDw", "sX4"):
        sems[name] = st_.enter_context(nc.semaphore(name))
        cnt[name] = 0

    ops = {k: [] for k in ("sync", "tensor", "vector", "scalar", "gpsimd")}

    def run(eng, fn):
        def f(e, _fn=fn):
            e._last = _fn(e)
        ops[eng].append(f)

    def inc(eng, name, k=1):
        h = sems[name]
        cnt[name] += k
        def f(e, _h=h, _k=k):
            e._last.then_inc(_h, _k)
        ops[eng].append(f)
        return cnt[name]

    def wait(eng, name, val):
        if val <= 0:
            return
        h = sems[name]
        def f(e, _h=h, _v=val):
            e.wait_ge(_h, _v)
        ops[eng].append(f)

    run("sync", lambda e: e.dma_start(out=cos_t, in_=cos_d)); inc("sync", "sConst", 16)
    run("sync", lambda e: e.dma_start(out=sin_t, in_=sin_d)); inc("sync", "sConst", 16)
    run("sync", lambda e: e.dma_start(out=mask_t, in_=masks_d)); inc("sync", "sConst", 16)
    run("sync", lambda e: e.dma_start(out=ones_f, in_=ones_d)); inc("sync", "sConst", 16)
    run("sync", lambda e: e.dma_start(out=ident_f, in_=ident_d)); inc("sync", "sConst", 16)
    for _eng in ("tensor", "vector", "scalar", "gpsimd"):
        wait(_eng, "sConst", cnt["sConst"])

    run("gpsimd", lambda e: e.dma_start(out=ones_b, in_=ones_d)); inc("gpsimd", "swg", 16)
    run("gpsimd", lambda e: e.dma_start(out=ident_b, in_=ident_d)); inc("gpsimd", "swg", 16)
    run("gpsimd", lambda e: e.dma_start(
        out=wqkv_sb, in_=wqkv_d)); inc("gpsimd", "swg", 16)
    run("gpsimd", lambda e: e.dma_start(
        out=gw_sb.rearrange("p (c j) -> p c j", c=NHC),
        in_=gwt_d.rearrange("(c p) j -> p c j", p=128))); inc("gpsimd", "swg", 16)
    W_SYNC = cnt["swg"]
    WO_MILE = W_SYNC + 16

    run("vector", lambda e: e.memset(small[:, C_QKMAX:C_QKMAX + 4], 0.0))
    run("vector", lambda e: e.drain())

    def p1_stage_a(i):
        b = i % 2
        S = _stats(b)
        xv = xt_sb[:, b * D:(b + 1) * D]
        xnv = xn_sb[:, (i % 3) * D:(i % 3 + 1) * D]
        if i >= 2:
            wait("sync", "sXNorm", i - 1)
        sxp = "sx%d" % b
        run("sync", lambda e, _i=i, _xv=xv: e.dma_start(
            out=_xv, in_=x_full[_i * 128:(_i + 1) * 128, :]))
        xmile = inc("sync", sxp, 16)

        wait("scalar", sxp, xmile)
        if i >= 2:
            wait("scalar", "sTr", 4 * (i - 1))
            wait("scalar", "sStat", i - 1)
        run("scalar", lambda e, _xv=xv, _xnv=xnv, _c=S["s2"]: e.activation(
            _xnv, _xv, AF.Square, accum_out=smc(_c)))
        inc("scalar", "sSq", 1)

        wait("vector", sxp, xmile)
        if i >= 2:
            wait("vector", "sXNorm", i - 1)
        run("vector", lambda e, _xv=xv, _c=S["s1"]: e.reduce_sum(smc(_c), _xv, axis=AX.X))
        run("vector", lambda e: e.drain())
        run("vector", lambda e, _a=S["mean"], _b=S["s1"]: e.tensor_scalar(
            smc(_a), smc(_b), 1.0 / D, None, ALU.mult))
        wait("vector", "sSq", cnt["sSq"])
        run("vector", lambda e, _a=S["var"], _b=S["s2"]: e.tensor_scalar(
            smc(_a), smc(_b), 1.0 / D, None, ALU.mult))
        run("vector", lambda e: e.drain())
        run("vector", lambda e, _m=S["mean"]: e.tensor_tensor(
            smc(C_NT), smc(_m), smc(_m), ALU.mult))
        run("vector", lambda e: e.drain())
        run("vector", lambda e, _v=S["var"]: e.tensor_tensor(
            smc(_v), smc(_v), smc(C_NT), ALU.subtract))
        run("vector", lambda e: e.drain())
        run("vector", lambda e, _v=S["var"]: e.tensor_scalar(
            smc(_v), smc(_v), EPS, None, ALU.add))
        run("vector", lambda e: e.drain())
        run("vector", lambda e, _v=S["var"]: e.reciprocal(smc(_v), smc(_v)))
        inc("vector", "sStat", 1)

    def p1_stage_b(i):
        b = i % 2
        S = _stats(b)
        xv = xt_sb[:, b * D:(b + 1) * D]
        xnv = xn_sb[:, (i % 3) * D:(i % 3 + 1) * D]
        xnTv = xnT_sb[:, b * NHC * 128:(b + 1) * NHC * 128]
        qkv = qkt_sb[:, b * 512:(b + 1) * 512]

        wait("scalar", "sStat", i + 1)
        run("scalar", lambda e, _v=S["var"], _r=S["rstd"]: e.activation(
            smc(_r), smc(_v), AF.Sqrt))
        run("scalar", lambda e: e.drain())
        run("scalar", lambda e, _m=S["mean"], _r=S["rstd"], _n=S["nmr"]: e.mul(
            smc(_n), smc(_m), smc(_r)))
        run("scalar", lambda e: e.drain())
        run("scalar", lambda e, _n=S["nmr"]: e.mul(smc(_n), smc(_n), -1.0))
        run("scalar", lambda e: e.drain())
        run("scalar", lambda e, _xv=xv, _xnv=xnv, _n=S["nmr"], _r=S["rstd"]: e.activation(
            _xnv, _xv, AF.Identity, bias=smc(_n), scale=smc(_r)))
        inc("scalar", "sXNorm", 1)
        run("scalar", lambda e: e.drain())

        if i == 0:
            wait("tensor", "swg", W_SYNC)
        wait("tensor", "sXNorm", i + 1)
        for gtr in range(4):
            if cnt["sTr"] >= 2:
                wait("tensor", "sTrE", cnt["sTr"] - 1)
            ptr = pb[6 + (gtr % 2)]
            for c4 in range(4):
                c = gtr * 4 + c4
                run("tensor", lambda e, _c=c, _c4=c4, _xnv=xnv, _p=ptr: e.matmul(
                    _p[:, _c4 * 128:(_c4 + 1) * 128], _xnv[:, _c * 128:(_c + 1) * 128],
                    ident_b[:], start=True, stop=True))
            inc("tensor", "sTr", 1)
            wait("scalar", "sTr", cnt["sTr"])
            if i >= 2 and gtr == 0:
                wait("scalar", "sPV", i - 1)
            run("scalar", lambda e, _xnTv=xnTv, _g=gtr, _p=ptr: e.copy(
                _xnTv[:, _g * 512:(_g + 1) * 512], _p[:]))
            inc("scalar", "sTrE", 1)

        wait("tensor", "sTrE", cnt["sTrE"])
        if i >= 2:
            wait("tensor", "sRope", i - 1)
        pqk = pb[i % 2]
        for c in range(NHC):
            run("tensor", lambda e, _c=c, _x=xnTv, _p=pqk: e.matmul(
                _p[:], _x[:, _c * 128:(_c + 1) * 128],
                wqkv_sb[:, _c * 768:_c * 768 + 512],
                start=(_c == 0), stop=(_c == NHC - 1)))
        inc("tensor", "sPQK", 1)
        pv = pb[2 + i % 2]
        if i >= 2:
            wait("tensor", "sVev", i - 1)
        for c in range(NHC):
            run("tensor", lambda e, _c=c, _x=xnTv, _p=pv: e.matmul(
                _p[:, 0:256], _x[:, _c * 128:(_c + 1) * 128],
                wqkv_sb[:, _c * 768 + 512:_c * 768 + 768],
                start=(_c == 0), stop=(_c == NHC - 1)))
        inc("tensor", "sPV", 1)

        wait("vector", "sPQK", cnt["sPQK"])
        if i >= 2:
            wait("vector", "sQT", i - 1)
        cosv = cos_t[:, i * 64:(i + 1) * 64]
        sinv = sin_t[:, i * 64:(i + 1) * 64]
        u1 = small[:, C_U1:C_U1 + 64]
        u2 = small[:, C_U2:C_U2 + 64]
        u3 = small[:, C_U3:C_U3 + 64]
        u4 = small[:, C_U4:C_U4 + 64]
        for hc in range(4):
            c0 = hc * 128
            p1 = pqk[:, c0:c0 + 64]
            p2 = pqk[:, c0 + 64:c0 + 128]
            run("vector", lambda e, _p=p1, _cv=cosv: e.tensor_tensor(u1, _p, _cv, ALU.mult))
            run("vector", lambda e, _p=p2, _sv=sinv: e.scalar_tensor_tensor(
                u2, _p, -1.0, _sv, ALU.mult, ALU.mult))
            run("vector", lambda e, _p=p2, _cv=cosv: e.tensor_tensor(u3, _p, _cv, ALU.mult))
            run("vector", lambda e, _p=p1, _sv=sinv: e.tensor_tensor(u4, _p, _sv, ALU.mult))
            run("vector", lambda e: e.drain())
            run("vector", lambda e, _q=qkv, _c0=c0: e.tensor_tensor(
                _q[:, _c0:_c0 + 64], u1, u2, ALU.add))
            run("vector", lambda e, _q=qkv, _c0=c0: e.tensor_tensor(
                _q[:, _c0 + 64:_c0 + 128], u3, u4, ALU.add))
            run("vector", lambda e: e.drain())
        inc("vector", "sRope", 1)
        run("vector", lambda e, _q=qkv: e.tensor_tensor(sqs_sb[:], _q, _q, ALU.mult))
        run("vector", lambda e: e.drain())
        for hc in range(4):
            run("vector", lambda e, _hc=hc: e.reduce_sum(
                smc(C_NC + _hc), sqs_sb[:, _hc * 128:(_hc + 1) * 128], axis=AX.X))
        run("vector", lambda e: e.drain())
        run("vector", lambda e: e.tensor_tensor(
            small[:, C_QKMAX:C_QKMAX + 4], small[:, C_QKMAX:C_QKMAX + 4],
            small[:, C_NC:C_NC + 4], ALU.max))
        run("vector", lambda e: e.drain())
        wait("vector", "sPV", cnt["sPV"])
        run("vector", lambda e, _p=pv, _i=i: e.tensor_copy(
            V_sb[:, _i * 256:(_i + 1) * 256], _p[:, 0:256]))
        inc("vector", "sVev", 1)

        wait("tensor", "sRope", cnt["sRope"])
        if i >= 1:
            wait("tensor", "sT2e", 4 * i)
        for hc in range(4):
            run("tensor", lambda e, _q=qkv, _hc=hc: e.matmul(
                pb[4][:, _hc * 128:(_hc + 1) * 128], _q[:, _hc * 128:(_hc + 1) * 128],
                ident_b[:], start=True, stop=True))
        inc("tensor", "sQT", 1)
        wait("vector", "sQT", cnt["sQT"])
        for hc in range(4):
            dst = Q_sb if hc < 2 else K_sb
            h = hc % 2
            run("vector", lambda e, _d=dst, _h=h, _i=i, _hc=hc: e.tensor_copy(
                _d[:, _h * T + _i * 128:_h * T + (_i + 1) * 128],
                pb[4][:, _hc * 128:(_hc + 1) * 128]))
            inc("vector", "sT2e", 1)

    for i in range(NTB + 1):
        if i < NTB:
            p1_stage_a(i)
        if i >= 1:
            p1_stage_b(i - 1)

    PH1_T2E = cnt["sT2e"]
    PH1_VEV = cnt["sVev"]

    if PH >= 2:
        for h in range(2):
            wait("tensor", "sVev", PH1_VEV)
            if h == 1:
                wait("tensor", "sNrm5", 1)
            run("tensor", lambda e, _h=h: e.transpose(
                pb[5][0:1, 0:128], small[:, C_QKMAX + _h:C_QKMAX + _h + 1], ident_f))
            run("tensor", lambda e, _h=h: e.transpose(
                pb[5][0:1, 128:256], small[:, C_QKMAX + 2 + _h:C_QKMAX + 3 + _h], ident_f))
            inc("tensor", "sNrm1", 1)
            wait("vector", "sNrm1", cnt["sNrm1"])
            run("vector", lambda e: e.reduce_max(smc(C_NT)[0:1, :], pb[5][0:1, 0:128], axis=AX.X))
            run("vector", lambda e: e.reduce_max(smc(C_NT + 1)[0:1, :], pb[5][0:1, 128:256], axis=AX.X))
            run("vector", lambda e: e.drain())
            run("vector", lambda e: e.tensor_tensor(
                smc(C_M2)[0:1, :], smc(C_NT)[0:1, :], smc(C_NT + 1)[0:1, :], ALU.mult))
            inc("vector", "sNrm2", 1)
            wait("scalar", "sNrm2", cnt["sNrm2"])
            run("scalar", lambda e: e.activation(smc(C_MVAL)[0:1, :], smc(C_M2)[0:1, :], AF.Sqrt))
            inc("scalar", "sNrm3", 1)
            wait("tensor", "sNrm3", cnt["sNrm3"])
            run("tensor", lambda e: e.matmul(
                pb[5][:, 256:257], ones_f[0:1, 0:128], smc(C_MVAL)[0:1, :],
                start=True, stop=True))
            inc("tensor", "sNrm4", 1)
            wait("vector", "sNrm4", cnt["sNrm4"])
            run("vector", lambda e, _h=h: e.tensor_scalar(
                small[:, C_BIAS + _h:C_BIAS + _h + 1], pb[5][:, 256:257],
                -SCALE, None, ALU.mult))
            inc("vector", "sNrm5", 1)

        chunks = []
        for h in range(2):
            for J in range(T // 512):
                chunks.append((h, J))
        chunks = chunks[:ACH] if ACH < len(chunks) else chunks
        gpair = 0
        smile = {}
        mmile = {}
        gp0 = {}

        def a_emit_S(ci, kt):
            h, J = chunks[ci]
            gp = gp0[ci] + kt // 2
            if gp >= 2:
                wait("tensor", "sE", gp - 1)
            pS = pb[2 * (gp % 2) + (kt % 2)]
            qs = slice(h * T + J * 512, h * T + (J + 1) * 512)
            run("tensor", lambda e, _pS=pS, _kt=kt, _h=h, _qs=qs: e.matmul(
                _pS[:], K_sb[:, _h * T + _kt * 128:_h * T + (_kt + 1) * 128],
                Q_sb[:, _qs], start=True, stop=True))
            smile[(ci, kt)] = inc("tensor", "sS", 1)
            if kt >= 4 * J:
                r = kt - 4 * J
                wait("vector", "sS", smile[(ci, kt)])
                run("vector", lambda e, _pS=pS, _r=r: e.tensor_tensor(
                    _pS[:], _pS[:], mask_t[:, _r * 512:(_r + 1) * 512], ALU.add))
                mmile[(ci, kt)] = inc("vector", "sM", 1)

        def a_prologue(ci):
            h, J = chunks[ci]
            gp0[ci] = (gp0[ci - 1] + 2 * (chunks[ci - 1][1] + 1)) if ci else 0
            for kt in range(min(4, 4 * (J + 1))):
                a_emit_S(ci, kt)

        def a_body(ci):
            h, J = chunks[ci]
            nkt = 4 * (J + 1)
            npair = nkt // 2
            biasv = small[:, C_BIAS + h:C_BIAS + h + 1]
            if ci > 0:
                wait("tensor", "sEvac", ci)
            for p in range(npair):
                gp = gp0[ci] + p
                k0, k1 = 2 * p, 2 * p + 1
                pSpair = [pb01, pb23][gp % 2]
                wait("scalar", "sS", smile[(ci, k1)])
                mm = max(mmile.get((ci, k0), 0), mmile.get((ci, k1), 0))
                if mm:
                    wait("scalar", "sM", mm)
                if gp >= 2:
                    wait("scalar", "sAVt", 2 * (gp - 1))
                pvp = P_sb[:, (gp % 2) * 1024:(gp % 2) * 1024 + 1024]
                run("scalar", lambda e, _pS=pSpair, _pv=pvp, _b=biasv: e.activation(
                    _pv, _pS[:], AF.Exp, bias=_b, scale=SCALE))
                emile = inc("scalar", "sE", 1)
                wait("tensor", "sE", emile)
                for kt in (k0, k1):
                    pvw = P_sb[:, (gp % 2) * 1024 + (kt % 2) * 512:
                               (gp % 2) * 1024 + (kt % 2) * 512 + 512]
                    run("tensor", lambda e, _pv=pvw, _kt=kt, _n=nkt: e.matmul(
                        pb[5][0:1, :], ones_b[:, 0:1], _pv,
                        start=(_kt == 0), stop=(_kt == _n - 1)))
                    run("tensor", lambda e, _pv=pvw, _kt=kt, _h=h, _n=nkt: e.matmul(
                        pb[4][:], V_sb[:, _kt * 256 + _h * 128:_kt * 256 + (_h + 1) * 128],
                        _pv, start=(_kt == 0), stop=(_kt == _n - 1)))
                    inc("tensor", "sAVt", 1)
                if p + 2 < npair:
                    a_emit_S(ci, 2 * p + 4)
                    a_emit_S(ci, 2 * p + 5)

        def a_tail(ci):
            h, J = chunks[ci]
            qs = slice(h * T + J * 512, h * T + (J + 1) * 512)
            wait("vector", "sAVt", 2 * (gp0[ci] + 2 * (J + 1)))
            run("vector", lambda e: e.reciprocal(recip_row, pb[5][0:1, :]))
            inc("vector", "sR", 1)
            wait("tensor", "sR", cnt["sR"])
            run("tensor", lambda e: e.matmul(
                pb[6][:], ones_f[0:1, 0:128], recip_row, start=True, stop=True))
            inc("tensor", "sRep", 1)
            wait("vector", "sRep", cnt["sRep"])
            run("vector", lambda e: e.tensor_copy(repl_sb[:], pb[6][:]))
            run("vector", lambda e: e.drain())
            run("vector", lambda e, _qs=qs: e.tensor_tensor(
                ao_sb[:, _qs], pb[4][:], repl_sb[:], ALU.mult))
            inc("vector", "sEvac", 1)

        if chunks:
            wait("tensor", "sT2e", PH1_T2E)
            wait("tensor", "sNrm5", 2)
            wait("scalar", "sNrm5", 2)
            a_prologue(0)
            for ci in range(len(chunks)):
                a_body(ci)
                if ci + 1 < len(chunks):
                    a_prologue(ci + 1)
                a_tail(ci)
        nchunk = len(chunks)

        PH2_EVAC = cnt["sEvac"]

    if PH >= 3:
        for tb in range(NTB):
            bs = 0 if tb % 2 == 0 else 4
            if tb == 0:
                wait("tensor", "sEvac", PH2_EVAC)
                wait("tensor", "swg", WO_MILE)
            if tb >= 2:
                wait("tensor", "sStg", cnt["sStg"] - 1)
            for dc in range(2):
                for jc in range(4):
                    run("tensor", lambda e, _tb=tb, _dc=dc, _jc=jc, _bs=bs: e.matmul(
                        pb[_bs + _jc][:],
                        ao_sb[:, _dc * T + _tb * 128:_dc * T + (_tb + 1) * 128],
                        wo_sb[:, _dc * D + _jc * 512:_dc * D + (_jc + 1) * 512],
                        start=(_dc == 0), stop=(_dc == 1)))
            inc("tensor", "sWo", 1)
            wait("scalar", "sWo", cnt["sWo"])
            sgp = "sStgD%d" % (tb % 2)
            if tb >= 2:
                wait("scalar", sgp, cnt[sgp])
            sg = stage_sb[:, (tb % 2) * D:(tb % 2) * D + D]
            for jc in range(4):
                run("scalar", lambda e, _sg=sg, _jc=jc, _bs=bs: e.copy(
                    _sg[:, _jc * 512:(_jc + 1) * 512], pb[_bs + _jc][:]))
            run("scalar", lambda e: e.drain())
            inc("scalar", "sStg", 1)
            wait("sync", "sStg", cnt["sStg"])
            run("sync", lambda e, _sg=sg, _tb=tb: e.dma_start(
                out=cc_in[_tb * 128:(_tb + 1) * 128, :], in_=_sg))
            inc("sync", sgp, 16)
        STG_FULL = (cnt["sStgD0"], cnt["sStgD1"])

    if PH >= 3:
        wait("gpsimd", "sPV", NTB)
        wait("gpsimd", "sVev", NTB)
        run("gpsimd", lambda e: e.dma_start(out=wo_sb, in_=wo_d))
        assert inc("gpsimd", "swg", 16) == WO_MILE

    if PH >= 4:
        wait("gpsimd", "sStgD0", STG_FULL[0])
        wait("gpsimd", "sStgD1", STG_FULL[1])
        run("gpsimd", lambda e: e.collective_compute(
            "ReduceScatter", ALU.add, replica_groups=[list(range(NCORES))],
            ins=[cc_in], outs=[cc_out]))
        inc("gpsimd", "sCC", 1)

    if PH >= 5:
        for i in range(NTB_OWN):
            b = i % 2
            S = _stats(b)
            xv = x3a_sb[:, b * D:(b + 1) * D]
            xnv = xn3a_sb[:, b * D:(b + 1) * D]
            if i >= 2:
                wait("sync", "sXN3", i - 1)
            sxp = "sx%d" % b
            run("sync", lambda e, _i=i, _xv=xv: e.dma_start(
                out=_xv, in_=x_own[_i * 128:(_i + 1) * 128, :]))
            xmile = inc("sync", sxp, 16)
            wait("scalar", sxp, xmile)
            if i >= 2:
                wait("scalar", "sTr3", 4 * (i - 1))
                wait("scalar", "sStat3", i - 1)
            run("scalar", lambda e, _xv=xv, _xnv=xnv, _c=S["s2"]: e.activation(
                _xnv, _xv, AF.Square, accum_out=smc(_c)))
            inc("scalar", "sSq3", 1)
            wait("vector", sxp, xmile)
            if i >= 2:
                wait("vector", "sXN3", i - 1)
            run("vector", lambda e, _xv=xv, _c=S["s1"]: e.reduce_sum(smc(_c), _xv, axis=AX.X))
            run("vector", lambda e: e.drain())
            run("vector", lambda e, _a=S["mean"], _b=S["s1"]: e.tensor_scalar(
                smc(_a), smc(_b), 1.0 / D, None, ALU.mult))
            wait("vector", "sSq3", cnt["sSq3"])
            run("vector", lambda e, _a=S["var"], _b=S["s2"]: e.tensor_scalar(
                smc(_a), smc(_b), 1.0 / D, None, ALU.mult))
            run("vector", lambda e: e.drain())
            run("vector", lambda e, _m=S["mean"]: e.tensor_tensor(
                smc(C_NT), smc(_m), smc(_m), ALU.mult))
            run("vector", lambda e: e.drain())
            run("vector", lambda e, _v=S["var"]: e.tensor_tensor(
                smc(_v), smc(_v), smc(C_NT), ALU.subtract))
            run("vector", lambda e: e.drain())
            run("vector", lambda e, _v=S["var"]: e.tensor_scalar(
                smc(_v), smc(_v), EPS, None, ALU.add))
            run("vector", lambda e: e.drain())
            run("vector", lambda e, _v=S["var"]: e.reciprocal(smc(_v), smc(_v)))
            inc("vector", "sStat3", 1)
            wait("scalar", "sStat3", cnt["sStat3"])
            run("scalar", lambda e, _v=S["var"], _r=S["rstd"]: e.activation(
                smc(_r), smc(_v), AF.Sqrt))
            run("scalar", lambda e: e.drain())
            run("scalar", lambda e, _m=S["mean"], _r=S["rstd"], _n=S["nmr"]: e.mul(
                smc(_n), smc(_m), smc(_r)))
            run("scalar", lambda e: e.drain())
            run("scalar", lambda e, _n=S["nmr"]: e.mul(smc(_n), smc(_n), -1.0))
            run("scalar", lambda e: e.drain())
            run("scalar", lambda e, _xv=xv, _xnv=xnv, _n=S["nmr"], _r=S["rstd"]: e.activation(
                _xnv, _xv, AF.Identity, bias=smc(_n), scale=smc(_r)))
            inc("scalar", "sXN3", 1)
            run("scalar", lambda e: e.drain())
            wait("tensor", "sXN3", cnt["sXN3"])
            if i == 0:
                wait("tensor", "sStg", NTB)
            for gtr in range(4):
                if cnt["sTr3"] >= 2:
                    wait("tensor", "sTrE3", cnt["sTr3"] - 1)
                ptr = pb[6 + (gtr % 2)]
                for c4 in range(4):
                    c = gtr * 4 + c4
                    run("tensor", lambda e, _c=c, _c4=c4, _xnv=xnv, _p=ptr: e.matmul(
                        _p[:, _c4 * 128:(_c4 + 1) * 128], _xnv[:, _c * 128:(_c + 1) * 128],
                        ident_b[:], start=True, stop=True))
                inc("tensor", "sTr3", 1)
                wait("scalar", "sTr3", cnt["sTr3"])
                run("scalar", lambda e, _g=gtr, _i=i, _p=ptr: e.copy(
                    xnTo_sb.rearrange("p (c t) -> p c t", c=NHC)[
                        :, _g * 4:(_g + 1) * 4, _i * 128:(_i + 1) * 128],
                    _p.rearrange("p (c f) -> p c f", c=4)))
                inc("scalar", "sTrE3", 1)
        PH3_T3E = cnt["sTrE3"]

    if PH >= 6:
        CL8, CE8, CW8, CG8 = 256, 320, 384, 448
        CNM, CSM, CT8 = 480, 488, 496
        lg8 = small[:, CL8:CL8 + 64]
        e8 = small[:, CE8:CE8 + 64]
        w8 = small[:, CW8:CW8 + 64]
        g8 = small[:, CG8:CG8 + 32]
        nm8 = small[:, CNM:CNM + 8]
        sm8 = small[:, CSM:CSM + 8]
        t8 = [small[:, CT8 + 8 * k:CT8 + 8 * k + 8] for k in range(4)]

        def lgn(n):
            return small[:, CL8 + n:CL8 + 64:8]

        def e8n(n):
            return small[:, CE8 + n:CE8 + 64:8]

        def w8n(n):
            return small[:, CW8 + n:CW8 + 64:8]

        def g8g(g):
            return small[:, CG8 + g:CG8 + 32:4]

        wait("tensor", "sTrE3", PH3_T3E)
        for tb in range(NTB_OWN):
            for c in range(NHC):
                run("tensor", lambda e, _c=c, _tb=tb: e.matmul(
                    pb[4][:, _tb * 8:_tb * 8 + 8],
                    xnTo_sb[:, _c * TOWN + _tb * 128:_c * TOWN + (_tb + 1) * 128],
                    gw_sb[:, _c * 8:(_c + 1) * 8],
                    start=(_c == 0), stop=(_c == NHC - 1)))
        inc("tensor", "sLg", 1)

        def vmaxtree(dst, view):
            run("vector", lambda e: e.tensor_tensor(t8[0], view(0), view(1), ALU.max))
            run("vector", lambda e: e.tensor_tensor(t8[1], view(2), view(3), ALU.max))
            run("vector", lambda e: e.tensor_tensor(t8[2], view(4), view(5), ALU.max))
            run("vector", lambda e: e.tensor_tensor(t8[3], view(6), view(7), ALU.max))
            run("vector", lambda e: e.drain())
            run("vector", lambda e: e.tensor_tensor(t8[0], t8[0], t8[1], ALU.max))
            run("vector", lambda e: e.tensor_tensor(t8[2], t8[2], t8[3], ALU.max))
            run("vector", lambda e: e.drain())
            run("vector", lambda e, _d=dst: e.tensor_tensor(_d, t8[0], t8[2], ALU.max))
            run("vector", lambda e: e.drain())

        wait("vector", "sLg", cnt["sLg"])
        run("vector", lambda e: e.tensor_copy(lg8, pb[4][:, 0:64]))
        inc("vector", "sLgC", 1)
        run("vector", lambda e: e.drain())
        vmaxtree(nm8, lgn)
        for n in range(8):
            run("vector", lambda e, _n=n: e.tensor_tensor(e8n(_n), lgn(_n), nm8, ALU.subtract))
        run("vector", lambda e: e.drain())
        inc("vector", "sWm", 1)
        wait("scalar", "sWm", cnt["sWm"])
        run("scalar", lambda e: e.activation(lg8, e8, AF.Exp))
        inc("scalar", "sWm", 1)
        wait("vector", "sWm", cnt["sWm"])
        run("vector", lambda e: e.tensor_tensor(t8[0], lgn(0), lgn(1), ALU.add))
        run("vector", lambda e: e.tensor_tensor(t8[1], lgn(2), lgn(3), ALU.add))
        run("vector", lambda e: e.tensor_tensor(t8[2], lgn(4), lgn(5), ALU.add))
        run("vector", lambda e: e.tensor_tensor(t8[3], lgn(6), lgn(7), ALU.add))
        run("vector", lambda e: e.drain())
        run("vector", lambda e: e.tensor_tensor(t8[0], t8[0], t8[1], ALU.add))
        run("vector", lambda e: e.tensor_tensor(t8[2], t8[2], t8[3], ALU.add))
        run("vector", lambda e: e.drain())
        run("vector", lambda e: e.tensor_tensor(sm8, t8[0], t8[2], ALU.add))
        run("vector", lambda e: e.drain())
        run("vector", lambda e: e.reciprocal(sm8, sm8))
        run("vector", lambda e: e.drain())
        for n in range(8):
            run("vector", lambda e, _n=n: e.tensor_tensor(e8n(_n), lgn(_n), sm8, ALU.mult))
        run("vector", lambda e: e.drain())
        run("vector", lambda e: e.tensor_tensor(
            g8, small[:, CE8:CE8 + 64:2], small[:, CE8 + 1:CE8 + 64:2], ALU.max))
        run("vector", lambda e: e.drain())
        run("vector", lambda e: e.tensor_tensor(t8[0], g8g(0), g8g(1), ALU.max))
        run("vector", lambda e: e.tensor_tensor(t8[1], g8g(0), g8g(1), ALU.min))
        run("vector", lambda e: e.tensor_tensor(t8[2], g8g(2), g8g(3), ALU.max))
        run("vector", lambda e: e.tensor_tensor(t8[3], g8g(2), g8g(3), ALU.min))
        run("vector", lambda e: e.drain())
        run("vector", lambda e: e.tensor_tensor(t8[0], t8[0], t8[2], ALU.min))
        run("vector", lambda e: e.tensor_tensor(t8[1], t8[1], t8[3], ALU.max))
        run("vector", lambda e: e.drain())
        run("vector", lambda e: e.tensor_tensor(t8[0], t8[0], t8[1], ALU.max))
        run("vector", lambda e: e.drain())
        for g in range(4):
            run("vector", lambda e, _g=g: e.tensor_tensor(g8g(_g), g8g(_g), t8[0], ALU.is_ge))
        run("vector", lambda e: e.drain())
        for n in range(8):
            run("vector", lambda e, _n=n: e.tensor_tensor(w8n(_n), e8n(_n), g8g(_n // 2), ALU.mult))
        run("vector", lambda e: e.drain())
        vmaxtree(t8[1], w8n)
        for n in range(8):
            run("vector", lambda e, _n=n: e.tensor_tensor(lgn(_n), w8n(_n), t8[1], ALU.is_ge))
        run("vector", lambda e: e.drain())
        for n in range(8):
            run("vector", lambda e, _n=n: e.scalar_tensor_tensor(
                lgn(_n), lgn(_n), NEG, w8n(_n), ALU.mult, ALU.add))
        run("vector", lambda e: e.drain())
        vmaxtree(t8[2], lgn)
        for n in range(8):
            run("vector", lambda e, _n=n: e.tensor_tensor(lgn(_n), w8n(_n), t8[2], ALU.is_ge))
        run("vector", lambda e: e.drain())
        for n in range(8):
            run("vector", lambda e, _n=n: e.tensor_tensor(w8n(_n), w8n(_n), lgn(_n), ALU.mult))
        run("vector", lambda e: e.drain())
        inc("vector", "sWmT", 1)
        wait("vector", "sStgD0", STG_FULL[0])
        wait("vector", "sStgD1", STG_FULL[1])
        for tb in range(NTB_OWN):
            wait("tensor", "sWmT", cnt["sWmT"])
            if tb >= 1:
                wait("tensor", "sWmTc", cnt["sWmTc"])
            for n in range(8):
                pbank = pb[5] if n < 4 else pb[6]
                run("tensor", lambda e, _pb=pbank, _n=n, _tb=tb: e.transpose(
                    _pb[0:1, (_n % 4) * 128:(_n % 4) * 128 + 128],
                    small[:, CW8 + _tb * 8 + _n:CW8 + _tb * 8 + _n + 1], ident_f))
            inc("tensor", "sWmT", 1)
            wait("vector", "sWmT", cnt["sWmT"])
            run("vector", lambda e, _tb=tb: e.tensor_copy(
                wmr_sb.rearrange("p (n t) -> p n t", n=8)[0:1, 0:4, _tb * 128:(_tb + 1) * 128],
                pb[5][0:1, 0:512].rearrange("p (n t) -> p n t", n=4)))
            run("vector", lambda e, _tb=tb: e.tensor_copy(
                wmr_sb.rearrange("p (n t) -> p n t", n=8)[0:1, 4:8, _tb * 128:(_tb + 1) * 128],
                pb[6][0:1, 0:512].rearrange("p (n t) -> p n t", n=4)))
            inc("vector", "sWmTc", 1)
        ROUTE_DONE = cnt["sWmTc"]

    if PH >= 7:
        GUP_DMAS = 2 * NFC
        DWN_PER_EXP = 4 * 2 * NFC

        def mile_gup(n, fc):
            g = n * NFC + fc
            return ("sGU%d" % (g % 2), 32 * (g // 2 + 1))

        def dwn_idx(n, jc, tbg, fc):
            return ((n * 4 + jc) * 2 + tbg) * NFC + fc

        def mile_dwn(n, jc, tbg, fc):
            d = dwn_idx(n, jc, tbg, fc)
            return ("sDN%d" % (d % 4), 16 * (d // 4 + 1))

        wait("gpsimd", "sTrE3", PH3_T3E)
        for n in range(NEXP):
            for fc in range(NFC):
                g = n * NFC + fc
                bgu = g % 2
                if g >= 2:
                    wait("gpsimd", "sU", 2 * (g - 1))
                run("gpsimd", lambda e, _n=n, _fc=fc, _b=bgu: e.dma_start(
                    out=gwb_sb[:, _b * NHC * 128:(_b + 1) * NHC * 128],
                    in_=gates_d[_n, _fc]))
                inc("gpsimd", "sGU%d" % (g % 2), 16)
                run("gpsimd", lambda e, _n=n, _fc=fc, _b=bgu: e.dma_start(
                    out=uwb_sb[:, _b * NHC * 128:(_b + 1) * NHC * 128],
                    in_=ups_d[_n, _fc]))
                inc("gpsimd", "sGU%d" % (g % 2), 16)
            for jc in range(4):
                for tbg in range(2):
                    for fc in range(NFC):
                        d = dwn_idx(n, jc, tbg, fc)
                        bd = d % 4
                        if d >= 4:
                            wait("gpsimd", "sDwn", d - 3)
                        run("gpsimd", lambda e, _n=n, _jc=jc, _fc=fc, _b=bd: e.dma_start(
                            out=dwb_sb[:, _b * 512:(_b + 1) * 512],
                            in_=downs_d[_n, _jc, _fc]))
                        inc("gpsimd", "sDN%d" % (d % 4), 16)

        for n in range(NEXP):
            routed = n < 8
            if routed:
                if n == 0:
                    wait("tensor", "sWmTc", ROUTE_DONE)
                else:
                    wait("tensor", "sAcc", cnt["sAcc"])
                for tc in range(2):
                    if n + tc > 0:
                        wait("tensor", "sWbC", cnt["sWbC"])
                    run("tensor", lambda e, _n=n, _tc=tc: e.matmul(
                        pb[6][:], ones_b[0:1, 0:128],
                        wmr_sb[0:1, _n * TOWN + _tc * 512:_n * TOWN + (_tc + 1) * 512],
                        start=True, stop=True))
                    inc("tensor", "sRepW", 1)
                    wait("vector", "sRepW", cnt["sRepW"])
                    run("vector", lambda e, _tc=tc: e.tensor_copy(
                        wbc_sb[:, _tc * 512:(_tc + 1) * 512], pb[6][:]))
                    inc("vector", "sWbC", 1)
                    run("vector", lambda e: e.drain())
            if n >= 1:
                wait("tensor", "sAcc", cnt["sAcc"])
            for fc in range(NFC):
                g = n * NFC + fc
                bgu = g % 2
                gw = gwb_sb[:, bgu * NHC * 128:(bgu + 1) * NHC * 128]
                uw = uwb_sb[:, bgu * NHC * 128:(bgu + 1) * NHC * 128]
                _s, _v = mile_gup(n, fc)
                wait("tensor", _s, _v)
                for tc in range(2):
                    if g + tc > 0:
                        wait("tensor", "sSil", cnt["sSil"])
                        wait("tensor", "sS3", cnt["sS3"])
                    for c in range(NHC):
                        run("tensor", lambda e, _c=c, _gw=gw, _tc=tc: e.matmul(
                            pb[0][:], _gw[:, _c * 128:(_c + 1) * 128],
                            xnTo_sb[:, _c * TOWN + _tc * 512:_c * TOWN + (_tc + 1) * 512],
                            start=(_c == 0), stop=(_c == NHC - 1)))
                    inc("tensor", "sG", 1)
                    for c in range(NHC):
                        run("tensor", lambda e, _c=c, _uw=uw, _tc=tc: e.matmul(
                            pb[1][:], _uw[:, _c * 128:(_c + 1) * 128],
                            xnTo_sb[:, _c * TOWN + _tc * 512:_c * TOWN + (_tc + 1) * 512],
                            start=(_c == 0), stop=(_c == NHC - 1)))
                    inc("tensor", "sU", 1)
                    wait("scalar", "sG", cnt["sG"])
                    sl = silu_sb[:, (tc % 2) * 512:(tc % 2) * 512 + 512]
                    run("scalar", lambda e, _sl=sl: e.activation(_sl, pb[0][:], AF.Silu))
                    inc("scalar", "sSil", 1)
                    wait("vector", "sU", cnt["sU"])
                    wait("vector", "sSil", cnt["sSil"])
                    if n >= 1 and fc == 0 and tc == 0:
                        wait("vector", "sDwn", n * 4 * NFC)
                    sv = s_sb[:, fc * TOWN + tc * 512:fc * TOWN + (tc + 1) * 512]
                    if routed:
                        run("vector", lambda e, _sv=sv, _tc=tc: e.tensor_tensor(
                            _sv, pb[1][:], wbc_sb[:, _tc * 512:(_tc + 1) * 512], ALU.mult))
                        run("vector", lambda e: e.drain())
                        run("vector", lambda e, _sv=sv, _sl=sl: e.tensor_tensor(
                            _sv, _sv, _sl, ALU.mult))
                    else:
                        run("vector", lambda e, _sv=sv, _sl=sl: e.tensor_tensor(
                            _sv, pb[1][:], _sl, ALU.mult))
                    run("vector", lambda e: e.drain())
                    inc("vector", "sS3", 1)
            SD_BASE = cnt["sS3"]
            if n == NEXP - 1:
                wait("sync", "sDwn", 8 * DWN_PER_EXP)
                wait("sync", "sCC", 1)
                wait("sync", "sU", cnt["sU"])
                x4m = {}
                apm = {}

                def p4_load(i4):
                    b4 = i4 % 2
                    run("sync", lambda e, _i=i4, _b=b4: e.dma_start(
                        out=x3b_sb[:, _b * D:(_b + 1) * D],
                        in_=x_own[_i * 128:(_i + 1) * 128, :]))
                    x4m[i4] = inc("sync", "sX4%d" % b4, 16)
                    run("sync", lambda e, _i=i4, _b=b4: e.dma_start(
                        out=apj_sb[:, _b * D:(_b + 1) * D],
                        in_=cc_out[_i * 128:(_i + 1) * 128, :]))
                    apm[i4] = inc("sync", "sApj%d" % b4, 16)

                p4_load(0)
                p4_load(1)
            for jc in range(4):
                for tbg in range(2):
                    for fc in range(NFC):
                        d = dwn_idx(n, jc, tbg, fc)
                        bd = d % 4
                        _s, _v = mile_dwn(n, jc, tbg, fc)
                        wait("tensor", _s, _v)
                        if jc == 0 and tbg == 0 and fc == 0:
                            wait("tensor", "sS3", SD_BASE)
                        if fc == 0 and cnt["sAcc"] >= 1:
                            wait("tensor", "sAcc", cnt["sAcc"] - 1)
                        for tb4 in range(4):
                            tb = tbg * 4 + tb4
                            run("tensor", lambda e, _tb=tb, _fc=fc, _b=bd: e.matmul(
                                pb[_tb][:],
                                s_sb[:, _fc * TOWN + _tb * 128:_fc * TOWN + (_tb + 1) * 128],
                                dwb_sb[:, _b * 512:(_b + 1) * 512],
                                start=(_fc == 0), stop=(_fc == NFC - 1)))
                        inc("tensor", "sDwn", 1)
                    wait("vector", "sDwn", cnt["sDwn"])
                    for tb4 in range(4):
                        tb = tbg * 4 + tb4
                        av = acc_sb[:, tb * D + jc * 512:tb * D + (jc + 1) * 512]
                        if n == 0:
                            run("vector", lambda e, _av=av, _tb=tb: e.tensor_copy(_av, pb[_tb][:]))
                        else:
                            run("vector", lambda e, _av=av, _tb=tb: e.tensor_tensor(
                                _av, pb[_tb][:], _av, ALU.add))
                    run("vector", lambda e: e.drain())
                    inc("vector", "sAcc", 1)
                if n == NEXP - 1 and jc == 3:
                    for tb in range(NTB_OWN):
                        b4 = tb % 2
                        wait("vector", "sX4%d" % b4, x4m[tb])
                        wait("vector", "sApj%d" % b4, apm[tb])
                        odp = "sOutD%d" % b4
                        if tb >= 2:
                            wait("vector", odp, 16 * (tb // 2))
                        ov = out_sb[:, b4 * D:(b4 + 1) * D]
                        run("vector", lambda e, _tb=tb, _ov=ov, _b=b4: e.tensor_tensor(
                            _ov, acc_sb[:, _tb * D:(_tb + 1) * D],
                            x3b_sb[:, _b * D:(_b + 1) * D], ALU.add))
                        run("vector", lambda e: e.drain())
                        run("vector", lambda e, _ov=ov, _b=b4: e.tensor_tensor(
                            _ov, _ov, apj_sb[:, _b * D:(_b + 1) * D], ALU.add))
                        inc("vector", "sOut", 1)
                        wait("sync", "sOut", cnt["sOut"])
                        run("sync", lambda e, _tb=tb, _ov=ov: e.dma_start(
                            out=y_d[_tb * 128:(_tb + 1) * 128, :], in_=_ov))
                        inc("sync", odp, 16)
                        if tb + 2 < NTB_OWN:
                            p4_load(tb + 2)
        FFN_ACC = cnt["sAcc"]
        wait("sync", "sOutD0", cnt["sOutD0"])
        wait("sync", "sOutD1", cnt["sOutD1"])

    with nc.Block() as blk:
        @blk.sync
        def _(e):
            for f in ops["sync"]:
                f(e)

        @blk.tensor
        def _(e):
            for f in ops["tensor"]:
                f(e)

        @blk.vector
        def _(e):
            for f in ops["vector"]:
                f(e)

        @blk.scalar
        def _(e):
            for f in ops["scalar"]:
                f(e)

        @blk.gpsimd
        def _(e):
            for f in ops["gpsimd"]:
                f(e)

    st_.close()
    return nc


def _host_inputs(L0_attn_x, L0_attn_w_qkv, L0_attn_w_o, L0_ffn_gate_weight,
                 L0_ffn_gate_w, L0_ffn_up_w, L0_ffn_down_w, L0_ffn_sh_gate_w,
                 L0_ffn_sh_up_w, L0_ffn_sh_down_w):
    x = np.ascontiguousarray(np.asarray(L0_attn_x, dtype=np.float32).reshape(T, D))
    wqkv = np.asarray(L0_attn_w_qkv, dtype=np.float32)
    wo = np.asarray(L0_attn_w_o, dtype=np.float32)
    gwt = np.ascontiguousarray(np.asarray(L0_ffn_gate_weight, np.float32).T)
    gate_w = np.asarray(L0_ffn_gate_w, np.float32)
    up_w = np.asarray(L0_ffn_up_w, np.float32)
    down_w = np.asarray(L0_ffn_down_w, np.float32)
    shg = np.asarray(L0_ffn_sh_gate_w, np.float32)
    shu = np.asarray(L0_ffn_sh_up_w, np.float32)
    shd = np.asarray(L0_ffn_sh_down_w, np.float32)

    gates = np.ascontiguousarray(np.concatenate([
        np.transpose(gate_w, (0, 2, 1)),
        np.transpose(shg.reshape(2, FEXP, D), (0, 2, 1)),
    ], axis=0))
    ups = np.ascontiguousarray(np.concatenate([
        np.transpose(up_w, (0, 2, 1)),
        np.transpose(shu.reshape(2, FEXP, D), (0, 2, 1)),
    ], axis=0))
    downs = np.ascontiguousarray(np.concatenate([
        down_w.transpose(0, 2, 1),
        shd.reshape(D, 2, FEXP).transpose(1, 2, 0),
    ], axis=0))

    inv_freq = 1.0 / (10000.0 ** (np.arange(0, HD, 2, dtype=np.float64) / HD))
    tpos = np.arange(T, dtype=np.float64)
    freqs = np.outer(tpos, inv_freq)
    cos = np.cos(freqs).astype(np.float32)
    sin = np.sin(freqs).astype(np.float32)
    cos_tok = np.ascontiguousarray(
        cos.reshape(NTB, 128, 64).transpose(1, 0, 2).reshape(128, NTB * 64))
    sin_tok = np.ascontiguousarray(
        sin.reshape(NTB, 128, 64).transpose(1, 0, 2).reshape(128, NTB * 64))

    masks = np.zeros((128, 4 * 512), np.float32)
    qf = np.arange(512)
    for r in range(4):
        for kp in range(128):
            masks[kp, r * 512:(r + 1) * 512] = np.where(r * 128 + kp > qf, NEG, 0.0)

    ones = np.ones((128, 128), np.float32)
    ident = np.eye(128, dtype=np.float32)

    gates = np.ascontiguousarray(
        gates.reshape(NEXP, NHC, 128, NFC, 128).transpose(0, 3, 2, 1, 4)
        .reshape(NEXP, NFC, 128, NHC * 128))
    ups = np.ascontiguousarray(
        ups.reshape(NEXP, NHC, 128, NFC, 128).transpose(0, 3, 2, 1, 4)
        .reshape(NEXP, NFC, 128, NHC * 128))
    downs = np.ascontiguousarray(
        downs.reshape(NEXP, NFC, 128, 4, 512).transpose(0, 3, 1, 2, 4))
    common = dict(gwt=gwt, gates=gates, ups=ups, downs=downs,
                  cos_tok=cos_tok, sin_tok=sin_tok, masks=masks,
                  ones_in=ones, ident_in=ident, x_full=x)
    in_maps = []
    for c in range(NCORES):
        hA = 2 * c
        wq = wqkv[:, hA * 128:hA * 128 + 256]
        wk = wqkv[:, D + hA * 128:D + hA * 128 + 256]
        wv = wqkv[:, 2 * D + hA * 128:2 * D + hA * 128 + 256]
        m = dict(common)
        wq3 = np.concatenate([wq, wk, wv], axis=1)
        m["wqkv"] = np.ascontiguousarray(
            wq3.reshape(NHC, 128, 768).transpose(1, 0, 2).reshape(128, NHC * 768))
        wo_s = wo[hA * 128:hA * 128 + 256, :]
        m["wo"] = np.ascontiguousarray(
            wo_s.reshape(2, 128, D).transpose(1, 0, 2).reshape(128, 2 * D))
        m["x_own"] = np.ascontiguousarray(x[c * TOWN:(c + 1) * TOWN, :])
        in_maps.append(m)
    return in_maps


_NC_CACHE = [None]
_BUILD_PHASES = 99
_ATTN_CHUNKS = 9999



def kernel(**inputs):
    in_maps = _host_inputs(**inputs)
    if _NC_CACHE[0] is None:
        _NC_CACHE[0] = build_nc()
    nc = _NC_CACHE[0]
    res = run_bass_kernel_spmd(nc, in_maps, core_ids=list(range(NCORES)))
    out = np.concatenate([np.asarray(res.results[c]["y"]) for c in range(NCORES)],
                         axis=0)
    return out.reshape(1, T, D)


# version 07 (final)
# speedup vs baseline: 1.0187x; 1.0187x over previous
import math
import numpy as np
from contextlib import ExitStack

import concourse.bass as bass
import concourse.mybir as mybir
from concourse.bass_utils import run_bass_kernel_spmd

F32 = mybir.dt.float32
BF16 = mybir.dt.bfloat16
AF = mybir.ActivationFunctionType
ALU = mybir.AluOpType
AX = mybir.AxisListType

T = 8192
D = 2048
HD = 128
NCORES = 8
TOWN = T // NCORES
NTB = T // 128
NTB_OWN = TOWN // 128
SCALE = 1.0 / math.sqrt(HD)
FEXP = 1408
NFC = FEXP // 128
NEXP = 10
NHC = D // 128
EPS = 1e-5
NEG = -1e30

SB0 = 16512
TOP = 229344

def _al(x, a=128):
    return (x + a - 1) // a * a

O_ones_f = SB0
O_ident_f = _al(O_ones_f + 128 * 4)
O_ones_b = _al(O_ident_f + 128 * 4)
O_cos = _al(O_ones_b + 128 * 2)
O_sin = _al(O_cos + 4096 * 4)
O_mask = _al(O_sin + 4096 * 4)
O_small = _al(O_mask + 4 * 512 * 4)
O_gw = _al(O_small + 1024 * 4)
O_identb = _al(O_gw + NHC * 8 * 2)
CONST_END = _al(O_identb + 128 * 2)

O_Q = CONST_END
O_K = _al(O_Q + 2 * T * 2)
O_V = _al(O_K + 2 * T * 2)
PH1_BASE = _al(O_V + NTB * 256 * 2)
O_xt = PH1_BASE
O_xn = _al(O_xt + 2 * D * 4)
O_xnT = _al(O_xn + 3 * D * 2)
O_qkt = _al(O_xnT + 2 * NHC * 128 * 2)
O_sqs = _al(O_qkt + 2 * 512 * 2)
O_wqkv = _al(O_sqs + 512 * 2)
PH1_END = _al(O_wqkv + NHC * 768 * 2)
assert PH1_END <= TOP, PH1_END
O_ao = PH1_BASE
O_P = _al(O_ao + 2 * T * 2)
O_repl = _al(O_P + 2 * 1024 * 2)
O_wo = _al(O_repl + 512 * 4)
O_stage = _al(O_wo + 2 * D * 2)
PH2_END = _al(O_stage + 2 * D * 4)
assert PH2_END <= TOP, PH2_END
O_acc = CONST_END
O_xnTo = _al(O_acc + NTB_OWN * D * 4)
PH3A = _al(O_xnTo + NHC * TOWN * 2)
O_ssb = PH3A
R1 = _al(O_ssb + NFC * TOWN * 2)
O_x3a = R1
O_xn3a = PH1_BASE
R1_END_A = _al(O_x3a + 2 * D * 4)
O_gwb = R1
O_uwb = _al(O_gwb + 2 * NHC * 128 * 2)
O_dwb = _al(O_uwb + 2 * NHC * 128 * 2)
O_silu = _al(O_dwb + 4 * 512 * 2)
O_wbc = _al(O_silu + 2 * 512 * 2)
R1_END_B = _al(O_wbc + TOWN * 2)
R1_END = max(R1_END_A, R1_END_B)
assert R1_END <= TOP, R1_END
O_x3b = O_ssb
O_apj = _al(O_x3b + 2 * D * 4)
O_out = _al(O_apj + 2 * D * 4)
O_wmrr = _al(O_out + 2 * D * 4)
PH4_END = _al(O_wmrr + 8 * TOWN * 2)
assert PH4_END <= TOP, PH4_END

def _stats(par):
    b = par * 8
    return dict(s1=b, s2=b + 1, mean=b + 2, var=b + 3, rstd=b + 4, nmr=b + 5)
C_QKMAX = 16
C_NT = 20
C_M2 = 24
C_MVAL = 25
C_BIAS = 26
C_LG = 32
C_SC1 = 40
C_SC2 = 48
C_GS = 56
C_RT = 60
C_RECIP = 64
C_U1 = 576
C_U2 = 640
C_NC = 704
C_U3 = 708
C_U4 = 772


def build_nc():
    PH = _BUILD_PHASES
    ACH = _ATTN_CHUNKS

    nc = bass.Bass("TRN2", target_bir_lowering=False)

    x_full = nc.dram_tensor("x_full", [T, D], F32, kind="ExternalInput").ap()
    x_own = nc.dram_tensor("x_own", [TOWN, D], F32, kind="ExternalInput").ap()
    wqkv_d = nc.dram_tensor("wqkv", [128, NHC * 768], F32, kind="ExternalInput").ap()
    wo_d = nc.dram_tensor("wo", [128, 2 * D], F32, kind="ExternalInput").ap()
    gwt_d = nc.dram_tensor("gwt", [D, 8], F32, kind="ExternalInput").ap()
    gates_d = nc.dram_tensor("gates", [NEXP, NFC, 128, NHC * 128], F32, kind="ExternalInput").ap()
    ups_d = nc.dram_tensor("ups", [NEXP, NFC, 128, NHC * 128], F32, kind="ExternalInput").ap()
    downs_d = nc.dram_tensor("downs", [NEXP, 4, NFC, 128, 512], F32, kind="ExternalInput").ap()
    cos_d = nc.dram_tensor("cos_tok", [128, 4096], F32, kind="ExternalInput").ap()
    sin_d = nc.dram_tensor("sin_tok", [128, 4096], F32, kind="ExternalInput").ap()
    masks_d = nc.dram_tensor("masks", [128, 4 * 512], F32, kind="ExternalInput").ap()
    ones_d = nc.dram_tensor("ones_in", [128, 128], F32, kind="ExternalInput").ap()
    ident_d = nc.dram_tensor("ident_in", [128, 128], F32, kind="ExternalInput").ap()
    y_d = nc.dram_tensor("y", [TOWN, D], F32, kind="ExternalOutput").ap()
    cc_in = nc.dram_tensor("cc_in", [T, D], F32, kind="Internal").ap()
    cc_out = nc.dram_tensor("cc_out", [TOWN, D], F32, kind="Internal").ap()

    def sb(name, shape, dtype, off):
        return nc.alloc_sbuf_tensor_at(name, shape, dtype, offset=off).ap()

    ones_f = sb("ones_f", [128, 128], F32, O_ones_f)
    ident_f = sb("ident_f", [128, 128], F32, O_ident_f)
    ones_b = sb("ones_b", [128, 128], BF16, O_ones_b)
    cos_t = sb("cos_t", [128, 4096], F32, O_cos)
    sin_t = sb("sin_t", [128, 4096], F32, O_sin)
    mask_t = sb("mask_t", [128, 4 * 512], F32, O_mask)
    small = sb("small", [128, 1024], F32, O_small)
    gw_sb = sb("gw_sb", [128, NHC * 8], BF16, O_gw)
    ident_b = sb("ident_b", [128, 128], BF16, O_identb)

    Q_sb = sb("Q_sb", [128, 2 * T], BF16, O_Q)
    K_sb = sb("K_sb", [128, 2 * T], BF16, O_K)
    V_sb = sb("V_sb", [128, NTB * 256], BF16, O_V)
    xt_sb = sb("xt_sb", [128, 2 * D], F32, O_xt)
    xn_sb = sb("xn_sb", [128, 3 * D], BF16, O_xn)
    xnT_sb = sb("xnT_sb", [128, 2 * NHC * 128], BF16, O_xnT)
    qkt_sb = sb("qkt_sb", [128, 2 * 512], BF16, O_qkt)
    sqs_sb = sb("sqs_sb", [128, 512], BF16, O_sqs)
    wqkv_sb = sb("wqkv_sb", [128, NHC * 768], BF16, O_wqkv)

    ao_sb = sb("ao_sb", [128, 2 * T], BF16, O_ao)
    wmr_sb = sb("wmr_sb", [1, 8 * TOWN], BF16, O_wmrr)
    P_sb = sb("P_sb", [128, 2 * 1024], BF16, O_P)
    repl_sb = sb("repl_sb", [128, 512], F32, O_repl)
    wo_sb = sb("wo_sb", [128, 2 * D], BF16, O_wo)
    stage_sb = sb("stage_sb", [128, 2 * D], F32, O_stage)

    acc_sb = sb("acc_sb", [128, NTB_OWN * D], F32, O_acc)
    xnTo_sb = sb("xnTo_sb", [128, NHC * TOWN], BF16, O_xnTo)
    s_sb = sb("s_sb", [128, NFC * TOWN], BF16, O_ssb)
    x3a_sb = sb("x3a_sb", [128, 2 * D], F32, O_x3a)
    xn3a_sb = sb("xn3a_sb", [128, 2 * D], BF16, O_xn3a)
    gwb_sb = sb("gwb_sb", [128, 2 * NHC * 128], BF16, O_gwb)
    uwb_sb = sb("uwb_sb", [128, 2 * NHC * 128], BF16, O_uwb)
    dwb_sb = sb("dwb_sb", [128, 4 * 512], BF16, O_dwb)
    silu_sb = sb("silu_sb", [128, 2 * 512], BF16, O_silu)
    wbc_sb = sb("wbc_sb", [128, TOWN], BF16, O_wbc)
    x3b_sb = sb("x3b_sb", [128, 2 * D], F32, O_x3b)
    apj_sb = sb("apj_sb", [128, 2 * D], F32, O_apj)
    out_sb = sb("out_sb", [128, 2 * D], F32, O_out)

    def smc(c, w=1):
        return small[:, c:c + w]
    recip_row = small[0:1, C_RECIP:C_RECIP + 512]

    pb01 = nc.alloc_psum_tensor("pb01", [128, 1024], F32).ap()
    pb23 = nc.alloc_psum_tensor("pb23", [128, 1024], F32).ap()
    pb_hi = [nc.alloc_psum_tensor(f"pb{i}", [128, 512], F32).ap() for i in range(4, 8)]
    pb = [pb01[:, 0:512], pb01[:, 512:1024], pb23[:, 0:512], pb23[:, 512:1024]] + pb_hi

    st_ = ExitStack()
    sems = {}
    cnt = {}
    for name in ("sx0", "sx1", "swg", "sSq", "sStat", "sXNorm",
                 "sT10", "sT11", "sT20", "sT21", "sT30", "sT31",
                 "sStgD0", "sStgD1", "sX40", "sX41", "sApj0", "sApj1",
                 "sOutD0", "sOutD1", "sGU0", "sGU1", "sDN0", "sDN1", "sDN2", "sDN3", "sTr", "sTrE", "sQT", "sT2e", "sTr3", "sTrE3",
                 "sConst", "sPQK", "sPV",
                 "sRope", "sT2", "sVev", "sNrm1", "sNrm2", "sNrm3", "sNrm4",
                 "sNrm5", "sS", "sM", "sE", "sAVt", "sR", "sRep",
                 "sEvac", "sWo", "sStg", "sStgD", "sCC", "sT3",
                 "sLg", "sLgC", "sWm", "sWmT", "sWmTc", "sRepW", "sWbC",
                 "sG", "sU", "sSil", "sS3", "sDwn", "sAcc", "sApj", "sOut",
                 "sOutD", "sX3", "sSq3", "sStat3", "sXN3", "sDw", "sX4"):
        sems[name] = st_.enter_context(nc.semaphore(name))
        cnt[name] = 0

    ops = {k: [] for k in ("sync", "tensor", "vector", "scalar", "gpsimd")}

    def run(eng, fn):
        def f(e, _fn=fn):
            e._last = _fn(e)
        ops[eng].append(f)

    def inc(eng, name, k=1):
        h = sems[name]
        cnt[name] += k
        def f(e, _h=h, _k=k):
            e._last.then_inc(_h, _k)
        ops[eng].append(f)
        return cnt[name]

    def wait(eng, name, val):
        if val <= 0:
            return
        h = sems[name]
        def f(e, _h=h, _v=val):
            e.wait_ge(_h, _v)
        ops[eng].append(f)

    run("sync", lambda e: e.dma_start(out=cos_t, in_=cos_d)); inc("sync", "sConst", 16)
    run("sync", lambda e: e.dma_start(out=sin_t, in_=sin_d)); inc("sync", "sConst", 16)
    run("sync", lambda e: e.dma_start(out=mask_t, in_=masks_d)); inc("sync", "sConst", 16)
    run("sync", lambda e: e.dma_start(out=ones_f, in_=ones_d)); inc("sync", "sConst", 16)
    run("sync", lambda e: e.dma_start(out=ident_f, in_=ident_d)); inc("sync", "sConst", 16)
    for _eng in ("tensor", "vector", "scalar", "gpsimd"):
        wait(_eng, "sConst", cnt["sConst"])

    run("gpsimd", lambda e: e.dma_start(out=ones_b, in_=ones_d)); inc("gpsimd", "swg", 16)
    run("gpsimd", lambda e: e.dma_start(out=ident_b, in_=ident_d)); inc("gpsimd", "swg", 16)
    run("gpsimd", lambda e: e.dma_start(
        out=wqkv_sb, in_=wqkv_d)); inc("gpsimd", "swg", 16)
    run("gpsimd", lambda e: e.dma_start(
        out=gw_sb.rearrange("p (c j) -> p c j", c=NHC),
        in_=gwt_d.rearrange("(c p) j -> p c j", p=128))); inc("gpsimd", "swg", 16)
    W_SYNC = cnt["swg"]
    WO_MILE = W_SYNC + 16

    run("vector", lambda e: e.memset(small[:, C_QKMAX:C_QKMAX + 4], 0.0))
    run("vector", lambda e: e.drain())

    def p1_stage_a(i):
        b = i % 2
        S = _stats(b)
        xv = xt_sb[:, b * D:(b + 1) * D]
        xnv = xn_sb[:, (i % 3) * D:(i % 3 + 1) * D]
        if i >= 2:
            wait("sync", "sXNorm", i - 1)
        sxp = "sx%d" % b
        run("sync", lambda e, _i=i, _xv=xv: e.dma_start(
            out=_xv, in_=x_full[_i * 128:(_i + 1) * 128, :]))
        xmile = inc("sync", sxp, 16)

        wait("scalar", sxp, xmile)
        if i >= 2:
            wait("scalar", "sTr", 4 * (i - 1))
            wait("scalar", "sStat", i - 1)
        run("scalar", lambda e, _xv=xv, _xnv=xnv, _c=S["s2"]: e.activation(
            _xnv, _xv, AF.Square, accum_out=smc(_c)))
        inc("scalar", "sSq", 1)

        wait("vector", sxp, xmile)
        if i >= 2:
            wait("vector", "sXNorm", i - 1)
        run("vector", lambda e, _xv=xv, _c=S["s1"]: e.reduce_sum(smc(_c), _xv, axis=AX.X))
        run("vector", lambda e: e.drain())
        run("vector", lambda e, _a=S["mean"], _b=S["s1"]: e.tensor_scalar(
            smc(_a), smc(_b), 1.0 / D, None, ALU.mult))
        wait("vector", "sSq", cnt["sSq"])
        run("vector", lambda e, _a=S["var"], _b=S["s2"]: e.tensor_scalar(
            smc(_a), smc(_b), 1.0 / D, None, ALU.mult))
        run("vector", lambda e: e.drain())
        run("vector", lambda e, _m=S["mean"]: e.tensor_tensor(
            smc(C_NT), smc(_m), smc(_m), ALU.mult))
        run("vector", lambda e: e.drain())
        run("vector", lambda e, _v=S["var"]: e.tensor_tensor(
            smc(_v), smc(_v), smc(C_NT), ALU.subtract))
        run("vector", lambda e: e.drain())
        run("vector", lambda e, _v=S["var"]: e.tensor_scalar(
            smc(_v), smc(_v), EPS, None, ALU.add))
        run("vector", lambda e: e.drain())
        run("vector", lambda e, _v=S["var"]: e.reciprocal(smc(_v), smc(_v)))
        inc("vector", "sStat", 1)

    def p1_stage_b(i):
        b = i % 2
        S = _stats(b)
        xv = xt_sb[:, b * D:(b + 1) * D]
        xnv = xn_sb[:, (i % 3) * D:(i % 3 + 1) * D]
        xnTv = xnT_sb[:, b * NHC * 128:(b + 1) * NHC * 128]
        qkv = qkt_sb[:, b * 512:(b + 1) * 512]

        wait("scalar", "sStat", i + 1)
        run("scalar", lambda e, _v=S["var"], _r=S["rstd"]: e.activation(
            smc(_r), smc(_v), AF.Sqrt))
        run("scalar", lambda e: e.drain())
        run("scalar", lambda e, _m=S["mean"], _r=S["rstd"], _n=S["nmr"]: e.mul(
            smc(_n), smc(_m), smc(_r)))
        run("scalar", lambda e: e.drain())
        run("scalar", lambda e, _n=S["nmr"]: e.mul(smc(_n), smc(_n), -1.0))
        run("scalar", lambda e: e.drain())
        run("scalar", lambda e, _xv=xv, _xnv=xnv, _n=S["nmr"], _r=S["rstd"]: e.activation(
            _xnv, _xv, AF.Identity, bias=smc(_n), scale=smc(_r)))
        inc("scalar", "sXNorm", 1)
        run("scalar", lambda e: e.drain())

        if i == 0:
            wait("tensor", "swg", W_SYNC)
        wait("tensor", "sXNorm", i + 1)
        for gtr in range(4):
            if cnt["sTr"] >= 2:
                wait("tensor", "sTrE", cnt["sTr"] - 1)
            ptr = pb[6 + (gtr % 2)]
            for c4 in range(4):
                c = gtr * 4 + c4
                run("tensor", lambda e, _c=c, _c4=c4, _xnv=xnv, _p=ptr: e.matmul(
                    _p[:, _c4 * 128:(_c4 + 1) * 128], _xnv[:, _c * 128:(_c + 1) * 128],
                    ident_b[:], start=True, stop=True))
            inc("tensor", "sTr", 1)
            wait("scalar", "sTr", cnt["sTr"])
            if i >= 2 and gtr == 0:
                wait("scalar", "sPV", i - 1)
            run("scalar", lambda e, _xnTv=xnTv, _g=gtr, _p=ptr: e.copy(
                _xnTv[:, _g * 512:(_g + 1) * 512], _p[:]))
            inc("scalar", "sTrE", 1)

        wait("tensor", "sTrE", cnt["sTrE"])
        if i >= 2:
            wait("tensor", "sRope", i - 1)
        pqk = pb[i % 2]
        for c in range(NHC):
            run("tensor", lambda e, _c=c, _x=xnTv, _p=pqk: e.matmul(
                _p[:], _x[:, _c * 128:(_c + 1) * 128],
                wqkv_sb[:, _c * 768:_c * 768 + 512],
                start=(_c == 0), stop=(_c == NHC - 1)))
        inc("tensor", "sPQK", 1)
        pv = pb[2 + i % 2]
        if i >= 2:
            wait("tensor", "sVev", i - 1)
        for c in range(NHC):
            run("tensor", lambda e, _c=c, _x=xnTv, _p=pv: e.matmul(
                _p[:, 0:256], _x[:, _c * 128:(_c + 1) * 128],
                wqkv_sb[:, _c * 768 + 512:_c * 768 + 768],
                start=(_c == 0), stop=(_c == NHC - 1)))
        inc("tensor", "sPV", 1)

        wait("vector", "sPQK", cnt["sPQK"])
        if i >= 2:
            wait("vector", "sQT", i - 1)
        cosv = cos_t[:, i * 64:(i + 1) * 64]
        sinv = sin_t[:, i * 64:(i + 1) * 64]
        u1 = small[:, C_U1:C_U1 + 64]
        u2 = small[:, C_U2:C_U2 + 64]
        u3 = small[:, C_U3:C_U3 + 64]
        u4 = small[:, C_U4:C_U4 + 64]
        for hc in range(4):
            c0 = hc * 128
            p1 = pqk[:, c0:c0 + 64]
            p2 = pqk[:, c0 + 64:c0 + 128]
            run("vector", lambda e, _p=p1, _cv=cosv: e.tensor_tensor(u1, _p, _cv, ALU.mult))
            run("vector", lambda e, _p=p2, _sv=sinv: e.scalar_tensor_tensor(
                u2, _p, -1.0, _sv, ALU.mult, ALU.mult))
            run("vector", lambda e, _p=p2, _cv=cosv: e.tensor_tensor(u3, _p, _cv, ALU.mult))
            run("vector", lambda e, _p=p1, _sv=sinv: e.tensor_tensor(u4, _p, _sv, ALU.mult))
            run("vector", lambda e: e.drain())
            run("vector", lambda e, _q=qkv, _c0=c0: e.tensor_tensor(
                _q[:, _c0:_c0 + 64], u1, u2, ALU.add))
            run("vector", lambda e, _q=qkv, _c0=c0: e.tensor_tensor(
                _q[:, _c0 + 64:_c0 + 128], u3, u4, ALU.add))
            run("vector", lambda e: e.drain())
        inc("vector", "sRope", 1)
        run("vector", lambda e, _q=qkv: e.tensor_tensor(sqs_sb[:], _q, _q, ALU.mult))
        run("vector", lambda e: e.drain())
        for hc in range(4):
            run("vector", lambda e, _hc=hc: e.reduce_sum(
                smc(C_NC + _hc), sqs_sb[:, _hc * 128:(_hc + 1) * 128], axis=AX.X))
        run("vector", lambda e: e.drain())
        run("vector", lambda e: e.tensor_tensor(
            small[:, C_QKMAX:C_QKMAX + 4], small[:, C_QKMAX:C_QKMAX + 4],
            small[:, C_NC:C_NC + 4], ALU.max))
        run("vector", lambda e: e.drain())
        wait("vector", "sPV", cnt["sPV"])
        run("vector", lambda e, _p=pv, _i=i: e.tensor_copy(
            V_sb[:, _i * 256:(_i + 1) * 256], _p[:, 0:256]))
        inc("vector", "sVev", 1)

        wait("tensor", "sRope", cnt["sRope"])
        if i >= 1:
            wait("tensor", "sT2e", 4 * i)
        for hc in range(4):
            run("tensor", lambda e, _q=qkv, _hc=hc: e.matmul(
                pb[4][:, _hc * 128:(_hc + 1) * 128], _q[:, _hc * 128:(_hc + 1) * 128],
                ident_b[:], start=True, stop=True))
        inc("tensor", "sQT", 1)
        wait("vector", "sQT", cnt["sQT"])
        for hc in range(4):
            dst = Q_sb if hc < 2 else K_sb
            h = hc % 2
            run("vector", lambda e, _d=dst, _h=h, _i=i, _hc=hc: e.tensor_copy(
                _d[:, _h * T + _i * 128:_h * T + (_i + 1) * 128],
                pb[4][:, _hc * 128:(_hc + 1) * 128]))
            inc("vector", "sT2e", 1)

    for i in range(NTB + 1):
        if i < NTB:
            p1_stage_a(i)
        if i >= 1:
            p1_stage_b(i - 1)

    PH1_T2E = cnt["sT2e"]
    PH1_VEV = cnt["sVev"]

    if PH >= 2:
        for h in range(2):
            wait("tensor", "sVev", PH1_VEV)
            if h == 1:
                wait("tensor", "sNrm5", 1)
            run("tensor", lambda e, _h=h: e.transpose(
                pb[5][0:1, 0:128], small[:, C_QKMAX + _h:C_QKMAX + _h + 1], ident_f))
            run("tensor", lambda e, _h=h: e.transpose(
                pb[5][0:1, 128:256], small[:, C_QKMAX + 2 + _h:C_QKMAX + 3 + _h], ident_f))
            inc("tensor", "sNrm1", 1)
            wait("vector", "sNrm1", cnt["sNrm1"])
            run("vector", lambda e: e.reduce_max(smc(C_NT)[0:1, :], pb[5][0:1, 0:128], axis=AX.X))
            run("vector", lambda e: e.reduce_max(smc(C_NT + 1)[0:1, :], pb[5][0:1, 128:256], axis=AX.X))
            run("vector", lambda e: e.drain())
            run("vector", lambda e: e.tensor_tensor(
                smc(C_M2)[0:1, :], smc(C_NT)[0:1, :], smc(C_NT + 1)[0:1, :], ALU.mult))
            inc("vector", "sNrm2", 1)
            wait("scalar", "sNrm2", cnt["sNrm2"])
            run("scalar", lambda e: e.activation(smc(C_MVAL)[0:1, :], smc(C_M2)[0:1, :], AF.Sqrt))
            inc("scalar", "sNrm3", 1)
            wait("tensor", "sNrm3", cnt["sNrm3"])
            run("tensor", lambda e: e.matmul(
                pb[5][:, 256:257], ones_f[0:1, 0:128], smc(C_MVAL)[0:1, :],
                start=True, stop=True))
            inc("tensor", "sNrm4", 1)
            wait("vector", "sNrm4", cnt["sNrm4"])
            run("vector", lambda e, _h=h: e.tensor_scalar(
                small[:, C_BIAS + _h:C_BIAS + _h + 1], pb[5][:, 256:257],
                -SCALE, None, ALU.mult))
            inc("vector", "sNrm5", 1)

        chunks = []
        for h in range(2):
            for J in range(T // 512):
                chunks.append((h, J))
        chunks = chunks[:ACH] if ACH < len(chunks) else chunks
        gpair = 0
        smile = {}
        mmile = {}
        gp0 = {}

        def a_emit_S(ci, kt):
            h, J = chunks[ci]
            gp = gp0[ci] + kt // 2
            if gp >= 2:
                wait("tensor", "sE", gp - 1)
            pS = pb[2 * (gp % 2) + (kt % 2)]
            qs = slice(h * T + J * 512, h * T + (J + 1) * 512)
            run("tensor", lambda e, _pS=pS, _kt=kt, _h=h, _qs=qs: e.matmul(
                _pS[:], K_sb[:, _h * T + _kt * 128:_h * T + (_kt + 1) * 128],
                Q_sb[:, _qs], start=True, stop=True))
            smile[(ci, kt)] = inc("tensor", "sS", 1)
            if kt >= 4 * J:
                r = kt - 4 * J
                wait("vector", "sS", smile[(ci, kt)])
                run("vector", lambda e, _pS=pS, _r=r: e.tensor_tensor(
                    _pS[:], _pS[:], mask_t[:, _r * 512:(_r + 1) * 512], ALU.add))
                mmile[(ci, kt)] = inc("vector", "sM", 1)

        def a_prologue(ci):
            h, J = chunks[ci]
            gp0[ci] = (gp0[ci - 1] + 2 * (chunks[ci - 1][1] + 1)) if ci else 0
            for kt in range(min(4, 4 * (J + 1))):
                a_emit_S(ci, kt)

        def a_body(ci):
            h, J = chunks[ci]
            nkt = 4 * (J + 1)
            npair = nkt // 2
            biasv = small[:, C_BIAS + h:C_BIAS + h + 1]
            if ci > 0:
                wait("tensor", "sEvac", ci)
            for p in range(npair):
                gp = gp0[ci] + p
                k0, k1 = 2 * p, 2 * p + 1
                pSpair = [pb01, pb23][gp % 2]
                wait("scalar", "sS", smile[(ci, k1)])
                mm = max(mmile.get((ci, k0), 0), mmile.get((ci, k1), 0))
                if mm:
                    wait("scalar", "sM", mm)
                if gp >= 2:
                    wait("scalar", "sAVt", 2 * (gp - 1))
                pvp = P_sb[:, (gp % 2) * 1024:(gp % 2) * 1024 + 1024]
                run("scalar", lambda e, _pS=pSpair, _pv=pvp, _b=biasv: e.activation(
                    _pv, _pS[:], AF.Exp, bias=_b, scale=SCALE))
                emile = inc("scalar", "sE", 1)
                wait("tensor", "sE", emile)
                for kt in (k0, k1):
                    pvw = P_sb[:, (gp % 2) * 1024 + (kt % 2) * 512:
                               (gp % 2) * 1024 + (kt % 2) * 512 + 512]
                    run("tensor", lambda e, _pv=pvw, _kt=kt, _n=nkt: e.matmul(
                        pb[5][0:1, :], ones_b[:, 0:1], _pv,
                        start=(_kt == 0), stop=(_kt == _n - 1)))
                    run("tensor", lambda e, _pv=pvw, _kt=kt, _h=h, _n=nkt: e.matmul(
                        pb[4][:], V_sb[:, _kt * 256 + _h * 128:_kt * 256 + (_h + 1) * 128],
                        _pv, start=(_kt == 0), stop=(_kt == _n - 1)))
                    inc("tensor", "sAVt", 1)
                if p + 2 < npair:
                    a_emit_S(ci, 2 * p + 4)
                    a_emit_S(ci, 2 * p + 5)

        def a_tail(ci):
            h, J = chunks[ci]
            qs = slice(h * T + J * 512, h * T + (J + 1) * 512)
            wait("vector", "sAVt", 2 * (gp0[ci] + 2 * (J + 1)))
            run("vector", lambda e: e.reciprocal(recip_row, pb[5][0:1, :]))
            inc("vector", "sR", 1)
            wait("tensor", "sR", cnt["sR"])
            run("tensor", lambda e: e.matmul(
                pb[6][:], ones_f[0:1, 0:128], recip_row, start=True, stop=True))
            inc("tensor", "sRep", 1)
            wait("vector", "sRep", cnt["sRep"])
            run("vector", lambda e: e.tensor_copy(repl_sb[:], pb[6][:]))
            run("vector", lambda e: e.drain())
            run("vector", lambda e, _qs=qs: e.tensor_tensor(
                ao_sb[:, _qs], pb[4][:], repl_sb[:], ALU.mult))
            inc("vector", "sEvac", 1)

        if chunks:
            wait("tensor", "sT2e", PH1_T2E)
            wait("tensor", "sNrm5", 2)
            wait("scalar", "sNrm5", 2)
            a_prologue(0)
            for ci in range(len(chunks)):
                a_body(ci)
                if ci + 1 < len(chunks):
                    a_prologue(ci + 1)
                a_tail(ci)
        nchunk = len(chunks)

        PH2_EVAC = cnt["sEvac"]

    if PH >= 3:
        for tb in range(NTB):
            bs = 0 if tb % 2 == 0 else 4
            if tb == 0:
                wait("tensor", "sEvac", PH2_EVAC)
                wait("tensor", "swg", WO_MILE)
            if tb >= 2:
                wait("tensor", "sStg", cnt["sStg"] - 1)
            for dc in range(2):
                for jc in range(4):
                    run("tensor", lambda e, _tb=tb, _dc=dc, _jc=jc, _bs=bs: e.matmul(
                        pb[_bs + _jc][:],
                        ao_sb[:, _dc * T + _tb * 128:_dc * T + (_tb + 1) * 128],
                        wo_sb[:, _dc * D + _jc * 512:_dc * D + (_jc + 1) * 512],
                        start=(_dc == 0), stop=(_dc == 1)))
            inc("tensor", "sWo", 1)
            wait("scalar", "sWo", cnt["sWo"])
            sgp = "sStgD%d" % (tb % 2)
            if tb >= 2:
                wait("scalar", sgp, cnt[sgp])
            sg = stage_sb[:, (tb % 2) * D:(tb % 2) * D + D]
            for jc in range(4):
                run("scalar", lambda e, _sg=sg, _jc=jc, _bs=bs: e.copy(
                    _sg[:, _jc * 512:(_jc + 1) * 512], pb[_bs + _jc][:]))
            run("scalar", lambda e: e.drain())
            inc("scalar", "sStg", 1)
            wait("sync", "sStg", cnt["sStg"])
            run("sync", lambda e, _sg=sg, _tb=tb: e.dma_start(
                out=cc_in[_tb * 128:(_tb + 1) * 128, :], in_=_sg))
            inc("sync", sgp, 16)
        STG_FULL = (cnt["sStgD0"], cnt["sStgD1"])

    if PH >= 3:
        wait("gpsimd", "sPV", NTB)
        wait("gpsimd", "sVev", NTB)
        run("gpsimd", lambda e: e.dma_start(out=wo_sb, in_=wo_d))
        assert inc("gpsimd", "swg", 16) == WO_MILE

    if PH >= 4:
        wait("gpsimd", "sStgD0", STG_FULL[0])
        wait("gpsimd", "sStgD1", STG_FULL[1])
        run("gpsimd", lambda e: e.collective_compute(
            "ReduceScatter", ALU.add, replica_groups=[list(range(NCORES))],
            ins=[cc_in], outs=[cc_out]))
        inc("gpsimd", "sCC", 1)

    if PH >= 5:
        for i in range(NTB_OWN):
            b = i % 2
            S = _stats(b)
            xv = x3a_sb[:, b * D:(b + 1) * D]
            xnv = xn3a_sb[:, b * D:(b + 1) * D]
            if i >= 2:
                wait("sync", "sXN3", i - 1)
            sxp = "sx%d" % b
            run("sync", lambda e, _i=i, _xv=xv: e.dma_start(
                out=_xv, in_=x_own[_i * 128:(_i + 1) * 128, :]))
            xmile = inc("sync", sxp, 16)
            wait("scalar", sxp, xmile)
            if i >= 2:
                wait("scalar", "sTr3", 4 * (i - 1))
                wait("scalar", "sStat3", i - 1)
            run("scalar", lambda e, _xv=xv, _xnv=xnv, _c=S["s2"]: e.activation(
                _xnv, _xv, AF.Square, accum_out=smc(_c)))
            inc("scalar", "sSq3", 1)
            wait("vector", sxp, xmile)
            if i >= 2:
                wait("vector", "sXN3", i - 1)
            run("vector", lambda e, _xv=xv, _c=S["s1"]: e.reduce_sum(smc(_c), _xv, axis=AX.X))
            run("vector", lambda e: e.drain())
            run("vector", lambda e, _a=S["mean"], _b=S["s1"]: e.tensor_scalar(
                smc(_a), smc(_b), 1.0 / D, None, ALU.mult))
            wait("vector", "sSq3", cnt["sSq3"])
            run("vector", lambda e, _a=S["var"], _b=S["s2"]: e.tensor_scalar(
                smc(_a), smc(_b), 1.0 / D, None, ALU.mult))
            run("vector", lambda e: e.drain())
            run("vector", lambda e, _m=S["mean"]: e.tensor_tensor(
                smc(C_NT), smc(_m), smc(_m), ALU.mult))
            run("vector", lambda e: e.drain())
            run("vector", lambda e, _v=S["var"]: e.tensor_tensor(
                smc(_v), smc(_v), smc(C_NT), ALU.subtract))
            run("vector", lambda e: e.drain())
            run("vector", lambda e, _v=S["var"]: e.tensor_scalar(
                smc(_v), smc(_v), EPS, None, ALU.add))
            run("vector", lambda e: e.drain())
            run("vector", lambda e, _v=S["var"]: e.reciprocal(smc(_v), smc(_v)))
            inc("vector", "sStat3", 1)
            wait("scalar", "sStat3", cnt["sStat3"])
            run("scalar", lambda e, _v=S["var"], _r=S["rstd"]: e.activation(
                smc(_r), smc(_v), AF.Sqrt))
            run("scalar", lambda e: e.drain())
            run("scalar", lambda e, _m=S["mean"], _r=S["rstd"], _n=S["nmr"]: e.mul(
                smc(_n), smc(_m), smc(_r)))
            run("scalar", lambda e: e.drain())
            run("scalar", lambda e, _n=S["nmr"]: e.mul(smc(_n), smc(_n), -1.0))
            run("scalar", lambda e: e.drain())
            run("scalar", lambda e, _xv=xv, _xnv=xnv, _n=S["nmr"], _r=S["rstd"]: e.activation(
                _xnv, _xv, AF.Identity, bias=smc(_n), scale=smc(_r)))
            inc("scalar", "sXN3", 1)
            run("scalar", lambda e: e.drain())
            wait("tensor", "sXN3", cnt["sXN3"])
            if i == 0:
                wait("tensor", "sStg", NTB)
            for gtr in range(4):
                if cnt["sTr3"] >= 2:
                    wait("tensor", "sTrE3", cnt["sTr3"] - 1)
                ptr = pb[6 + (gtr % 2)]
                for c4 in range(4):
                    c = gtr * 4 + c4
                    run("tensor", lambda e, _c=c, _c4=c4, _xnv=xnv, _p=ptr: e.matmul(
                        _p[:, _c4 * 128:(_c4 + 1) * 128], _xnv[:, _c * 128:(_c + 1) * 128],
                        ident_b[:], start=True, stop=True))
                inc("tensor", "sTr3", 1)
                wait("scalar", "sTr3", cnt["sTr3"])
                run("scalar", lambda e, _g=gtr, _i=i, _p=ptr: e.copy(
                    xnTo_sb.rearrange("p (c t) -> p c t", c=NHC)[
                        :, _g * 4:(_g + 1) * 4, _i * 128:(_i + 1) * 128],
                    _p.rearrange("p (c f) -> p c f", c=4)))
                inc("scalar", "sTrE3", 1)
        PH3_T3E = cnt["sTrE3"]

    if PH >= 6:
        for tb in range(NTB_OWN):
            if tb == 0:
                wait("tensor", "sTrE3", PH3_T3E)
            else:
                wait("tensor", "sLgC", cnt["sLgC"])
            for c in range(NHC):
                run("tensor", lambda e, _c=c, _tb=tb: e.matmul(
                    pb[4][:, 0:8],
                    xnTo_sb[:, _c * TOWN + _tb * 128:_c * TOWN + (_tb + 1) * 128],
                    gw_sb[:, _c * 8:(_c + 1) * 8],
                    start=(_c == 0), stop=(_c == NHC - 1)))
            inc("tensor", "sLg", 1)
            wait("vector", "sLg", cnt["sLg"])
            lg = small[:, C_LG:C_LG + 8]
            sc1 = small[:, C_SC1:C_SC1 + 8]
            sc2 = small[:, C_SC2:C_SC2 + 8]
            gs = small[:, C_GS:C_GS + 4]
            t1 = smc(C_RT)
            t2 = smc(C_RT + 1)
            t3 = smc(C_RT + 2)
            t4 = smc(C_RT + 3)
            run("vector", lambda e: e.tensor_copy(lg, pb[4][:, 0:8]))
            inc("vector", "sLgC", 1)
            run("vector", lambda e: e.drain())
            run("vector", lambda e: e.reduce_max(t1, lg, axis=AX.X, negate=True))
            inc("vector", "sWm", 1)
            wait("scalar", "sWm", cnt["sWm"])
            run("scalar", lambda e: e.activation(sc1, lg, AF.Exp, bias=t1, scale=1.0))
            inc("scalar", "sWm", 1)
            wait("vector", "sWm", cnt["sWm"])
            run("vector", lambda e: e.reduce_sum(t2, sc1, axis=AX.X))
            run("vector", lambda e: e.drain())
            run("vector", lambda e: e.reciprocal(t3, t2))
            run("vector", lambda e: e.drain())
            run("vector", lambda e: e.tensor_scalar(sc1, sc1, t3, None, ALU.mult))
            run("vector", lambda e: e.drain())
            run("vector", lambda e: e.tensor_tensor(gs, sc1[:, 0:8:2], sc1[:, 1:8:2], ALU.max))
            run("vector", lambda e: e.drain())
            run("vector", lambda e: e.tensor_tensor(t1, gs[:, 0:1], gs[:, 1:2], ALU.max))
            run("vector", lambda e: e.tensor_tensor(t2, gs[:, 0:1], gs[:, 1:2], ALU.min))
            run("vector", lambda e: e.tensor_tensor(t3, gs[:, 2:3], gs[:, 3:4], ALU.max))
            run("vector", lambda e: e.tensor_tensor(t4, gs[:, 2:3], gs[:, 3:4], ALU.min))
            run("vector", lambda e: e.drain())
            run("vector", lambda e: e.tensor_tensor(t1, t1, t3, ALU.min))
            run("vector", lambda e: e.tensor_tensor(t2, t2, t4, ALU.max))
            run("vector", lambda e: e.drain())
            run("vector", lambda e: e.tensor_tensor(t1, t1, t2, ALU.max))
            run("vector", lambda e: e.drain())
            run("vector", lambda e: e.tensor_scalar(gs, gs, t1, None, ALU.is_ge))
            run("vector", lambda e: e.drain())
            run("vector", lambda e: e.tensor_copy(sc2[:, 0:8:2], gs))
            run("vector", lambda e: e.tensor_copy(sc2[:, 1:8:2], gs))
            run("vector", lambda e: e.drain())
            run("vector", lambda e: e.tensor_tensor(sc2, sc2, sc1, ALU.mult))
            run("vector", lambda e: e.drain())
            run("vector", lambda e: e.reduce_max(t1, sc2, axis=AX.X))
            run("vector", lambda e: e.drain())
            run("vector", lambda e: e.tensor_scalar(lg, sc2, t1, None, ALU.is_ge))
            run("vector", lambda e: e.drain())
            run("vector", lambda e: e.scalar_tensor_tensor(lg, lg, NEG, sc2, ALU.mult, ALU.add))
            run("vector", lambda e: e.drain())
            run("vector", lambda e: e.reduce_max(t2, lg, axis=AX.X))
            run("vector", lambda e: e.drain())
            run("vector", lambda e: e.tensor_scalar(lg, sc2, t2, None, ALU.is_ge))
            run("vector", lambda e: e.drain())
            run("vector", lambda e: e.tensor_tensor(sc2, sc2, lg, ALU.mult))
            inc("vector", "sWmT", 1)
            wait("tensor", "sWmT", cnt["sWmT"])
            if tb >= 1:
                wait("tensor", "sWmTc", cnt["sWmTc"])
            for n in range(8):
                pbank = pb[5] if n < 4 else pb[6]
                run("tensor", lambda e, _pb=pbank, _n=n: e.transpose(
                    _pb[0:1, (_n % 4) * 128:(_n % 4) * 128 + 128],
                    small[:, C_SC2 + _n:C_SC2 + _n + 1], ident_f))
            inc("tensor", "sWmT", 1)
            wait("vector", "sWmT", cnt["sWmT"])
            if tb == 0:
                wait("vector", "sStgD0", STG_FULL[0])
                wait("vector", "sStgD1", STG_FULL[1])
            run("vector", lambda e, _tb=tb: e.tensor_copy(
                wmr_sb.rearrange("p (n t) -> p n t", n=8)[0:1, 0:4, _tb * 128:(_tb + 1) * 128],
                pb[5][0:1, 0:512].rearrange("p (n t) -> p n t", n=4)))
            run("vector", lambda e, _tb=tb: e.tensor_copy(
                wmr_sb.rearrange("p (n t) -> p n t", n=8)[0:1, 4:8, _tb * 128:(_tb + 1) * 128],
                pb[6][0:1, 0:512].rearrange("p (n t) -> p n t", n=4)))
            inc("vector", "sWmTc", 1)
        ROUTE_DONE = cnt["sWmTc"]

    if PH >= 7:
        GUP_DMAS = 2 * NFC
        DWN_DMAS = 4 * NFC

        def mile_gup(n, fc):
            g = n * NFC + fc
            return ("sGU%d" % (g % 2), 32 * (g // 2 + 1))

        def mile_dwn(n, jc, fc):
            d = n * 4 * NFC + jc * NFC + fc
            return ("sDN%d" % (d % 4), 16 * (d // 4 + 1))

        wait("gpsimd", "sTrE3", PH3_T3E)
        for n in range(NEXP):
            for fc in range(NFC):
                g = n * NFC + fc
                bgu = g % 2
                if g >= 2:
                    wait("gpsimd", "sU", 2 * (g - 1))
                run("gpsimd", lambda e, _n=n, _fc=fc, _b=bgu: e.dma_start(
                    out=gwb_sb[:, _b * NHC * 128:(_b + 1) * NHC * 128],
                    in_=gates_d[_n, _fc]))
                inc("gpsimd", "sGU%d" % (g % 2), 16)
                run("gpsimd", lambda e, _n=n, _fc=fc, _b=bgu: e.dma_start(
                    out=uwb_sb[:, _b * NHC * 128:(_b + 1) * NHC * 128],
                    in_=ups_d[_n, _fc]))
                inc("gpsimd", "sGU%d" % (g % 2), 16)
            for jc in range(4):
                for fc in range(NFC):
                    d = n * 4 * NFC + jc * NFC + fc
                    bd = d % 4
                    if d >= 4:
                        wait("gpsimd", "sDwn", d - 3)
                    run("gpsimd", lambda e, _n=n, _jc=jc, _fc=fc, _b=bd: e.dma_start(
                        out=dwb_sb[:, _b * 512:(_b + 1) * 512],
                        in_=downs_d[_n, _jc, _fc]))
                    inc("gpsimd", "sDN%d" % (d % 4), 16)

        for n in range(NEXP):
            routed = n < 8
            if routed:
                if n == 0:
                    wait("tensor", "sWmTc", ROUTE_DONE)
                else:
                    wait("tensor", "sAcc", cnt["sAcc"])
                for tc in range(2):
                    if n + tc > 0:
                        wait("tensor", "sWbC", cnt["sWbC"])
                    run("tensor", lambda e, _n=n, _tc=tc: e.matmul(
                        pb[6][:], ones_b[0:1, 0:128],
                        wmr_sb[0:1, _n * TOWN + _tc * 512:_n * TOWN + (_tc + 1) * 512],
                        start=True, stop=True))
                    inc("tensor", "sRepW", 1)
                    wait("vector", "sRepW", cnt["sRepW"])
                    run("vector", lambda e, _tc=tc: e.tensor_copy(
                        wbc_sb[:, _tc * 512:(_tc + 1) * 512], pb[6][:]))
                    inc("vector", "sWbC", 1)
                    run("vector", lambda e: e.drain())
            if n >= 1:
                wait("tensor", "sAcc", cnt["sAcc"])
            for fc in range(NFC):
                g = n * NFC + fc
                bgu = g % 2
                gw = gwb_sb[:, bgu * NHC * 128:(bgu + 1) * NHC * 128]
                uw = uwb_sb[:, bgu * NHC * 128:(bgu + 1) * NHC * 128]
                _s, _v = mile_gup(n, fc)
                wait("tensor", _s, _v)
                for tc in range(2):
                    if g + tc > 0:
                        wait("tensor", "sSil", cnt["sSil"])
                        wait("tensor", "sS3", cnt["sS3"])
                    for c in range(NHC):
                        run("tensor", lambda e, _c=c, _gw=gw, _tc=tc: e.matmul(
                            pb[0][:], _gw[:, _c * 128:(_c + 1) * 128],
                            xnTo_sb[:, _c * TOWN + _tc * 512:_c * TOWN + (_tc + 1) * 512],
                            start=(_c == 0), stop=(_c == NHC - 1)))
                    inc("tensor", "sG", 1)
                    for c in range(NHC):
                        run("tensor", lambda e, _c=c, _uw=uw, _tc=tc: e.matmul(
                            pb[1][:], _uw[:, _c * 128:(_c + 1) * 128],
                            xnTo_sb[:, _c * TOWN + _tc * 512:_c * TOWN + (_tc + 1) * 512],
                            start=(_c == 0), stop=(_c == NHC - 1)))
                    inc("tensor", "sU", 1)
                    wait("scalar", "sG", cnt["sG"])
                    sl = silu_sb[:, (tc % 2) * 512:(tc % 2) * 512 + 512]
                    run("scalar", lambda e, _sl=sl: e.activation(_sl, pb[0][:], AF.Silu))
                    inc("scalar", "sSil", 1)
                    wait("vector", "sU", cnt["sU"])
                    wait("vector", "sSil", cnt["sSil"])
                    if n >= 1 and fc == 0 and tc == 0:
                        wait("vector", "sDwn", n * 4 * NFC)
                    sv = s_sb[:, fc * TOWN + tc * 512:fc * TOWN + (tc + 1) * 512]
                    if routed:
                        run("vector", lambda e, _sv=sv, _tc=tc: e.tensor_tensor(
                            _sv, pb[1][:], wbc_sb[:, _tc * 512:(_tc + 1) * 512], ALU.mult))
                        run("vector", lambda e: e.drain())
                        run("vector", lambda e, _sv=sv, _sl=sl: e.tensor_tensor(
                            _sv, _sv, _sl, ALU.mult))
                    else:
                        run("vector", lambda e, _sv=sv, _sl=sl: e.tensor_tensor(
                            _sv, pb[1][:], _sl, ALU.mult))
                    run("vector", lambda e: e.drain())
                    inc("vector", "sS3", 1)
            SD_BASE = cnt["sS3"]
            for jc in range(4):
                for fc in range(NFC):
                    d = n * 4 * NFC + jc * NFC + fc
                    bd = d % 4
                    _s, _v = mile_dwn(n, jc, fc)
                    wait("tensor", _s, _v)
                    if jc == 0 and fc == 0:
                        wait("tensor", "sS3", SD_BASE)
                    accbase = cnt["sAcc"]
                    for tb in range(NTB_OWN):
                        if fc == 0 and accbase >= 8:
                            wait("tensor", "sAcc", accbase - 8 + tb + 1)
                        run("tensor", lambda e, _tb=tb, _fc=fc, _b=bd: e.matmul(
                            pb[_tb][:],
                            s_sb[:, _fc * TOWN + _tb * 128:_fc * TOWN + (_tb + 1) * 128],
                            dwb_sb[:, _b * 512:(_b + 1) * 512],
                            start=(_fc == 0), stop=(_fc == NFC - 1)))
                    inc("tensor", "sDwn", 1)
                wait("vector", "sDwn", cnt["sDwn"])
                for tb in range(NTB_OWN):
                    av = acc_sb[:, tb * D + jc * 512:tb * D + (jc + 1) * 512]
                    if n == 0:
                        run("vector", lambda e, _av=av, _tb=tb: e.tensor_copy(_av, pb[_tb][:]))
                    else:
                        run("vector", lambda e, _av=av, _tb=tb: e.tensor_tensor(
                            _av, pb[_tb][:], _av, ALU.add))
                    inc("vector", "sAcc", 1)
                run("vector", lambda e: e.drain())
        FFN_ACC = cnt["sAcc"]

    if PH >= 8:
        for i in range(NTB_OWN):
            b = i % 2
            if i == 0:
                wait("sync", "sAcc", FFN_ACC)
                wait("sync", "sCC", 1)
            if i >= 2:
                wait("sync", "sOut", i - 1)
            run("sync", lambda e, _i=i, _b=b: e.dma_start(
                out=x3b_sb[:, _b * D:(_b + 1) * D], in_=x_own[_i * 128:(_i + 1) * 128, :]))
            x4m = inc("sync", "sX4%d" % b, 16)
            run("sync", lambda e, _i=i, _b=b: e.dma_start(
                out=apj_sb[:, _b * D:(_b + 1) * D], in_=cc_out[_i * 128:(_i + 1) * 128, :]))
            apm = inc("sync", "sApj%d" % b, 16)
            odp = "sOutD%d" % b
            if i >= 2:
                wait("vector", odp, cnt[odp])
            wait("vector", "sX4%d" % b, x4m)
            wait("vector", "sApj%d" % b, apm)
            ov = out_sb[:, b * D:(b + 1) * D]
            run("vector", lambda e, _i=i, _b=b, _ov=ov: e.tensor_tensor(
                _ov, acc_sb[:, _i * D:(_i + 1) * D], x3b_sb[:, _b * D:(_b + 1) * D], ALU.add))
            run("vector", lambda e: e.drain())
            run("vector", lambda e, _b=b, _ov=ov: e.tensor_tensor(
                _ov, _ov, apj_sb[:, _b * D:(_b + 1) * D], ALU.add))
            inc("vector", "sOut", 1)
            wait("sync", "sOut", cnt["sOut"])
            run("sync", lambda e, _i=i, _ov=ov: e.dma_start(
                out=y_d[_i * 128:(_i + 1) * 128, :], in_=_ov))
            inc("sync", odp, 16)
        wait("sync", "sOutD0", cnt["sOutD0"])
        wait("sync", "sOutD1", cnt["sOutD1"])

    with nc.Block() as blk:
        @blk.sync
        def _(e):
            for f in ops["sync"]:
                f(e)

        @blk.tensor
        def _(e):
            for f in ops["tensor"]:
                f(e)

        @blk.vector
        def _(e):
            for f in ops["vector"]:
                f(e)

        @blk.scalar
        def _(e):
            for f in ops["scalar"]:
                f(e)

        @blk.gpsimd
        def _(e):
            for f in ops["gpsimd"]:
                f(e)

    st_.close()
    return nc


def _host_inputs(L0_attn_x, L0_attn_w_qkv, L0_attn_w_o, L0_ffn_gate_weight,
                 L0_ffn_gate_w, L0_ffn_up_w, L0_ffn_down_w, L0_ffn_sh_gate_w,
                 L0_ffn_sh_up_w, L0_ffn_sh_down_w):
    x = np.ascontiguousarray(np.asarray(L0_attn_x, dtype=np.float32).reshape(T, D))
    wqkv = np.asarray(L0_attn_w_qkv, dtype=np.float32)
    wo = np.asarray(L0_attn_w_o, dtype=np.float32)
    gwt = np.ascontiguousarray(np.asarray(L0_ffn_gate_weight, np.float32).T)
    gate_w = np.asarray(L0_ffn_gate_w, np.float32)
    up_w = np.asarray(L0_ffn_up_w, np.float32)
    down_w = np.asarray(L0_ffn_down_w, np.float32)
    shg = np.asarray(L0_ffn_sh_gate_w, np.float32)
    shu = np.asarray(L0_ffn_sh_up_w, np.float32)
    shd = np.asarray(L0_ffn_sh_down_w, np.float32)

    gates = np.ascontiguousarray(np.concatenate([
        np.transpose(gate_w, (0, 2, 1)),
        np.transpose(shg.reshape(2, FEXP, D), (0, 2, 1)),
    ], axis=0))
    ups = np.ascontiguousarray(np.concatenate([
        np.transpose(up_w, (0, 2, 1)),
        np.transpose(shu.reshape(2, FEXP, D), (0, 2, 1)),
    ], axis=0))
    downs = np.ascontiguousarray(np.concatenate([
        down_w.transpose(0, 2, 1),
        shd.reshape(D, 2, FEXP).transpose(1, 2, 0),
    ], axis=0))

    inv_freq = 1.0 / (10000.0 ** (np.arange(0, HD, 2, dtype=np.float64) / HD))
    tpos = np.arange(T, dtype=np.float64)
    freqs = np.outer(tpos, inv_freq)
    cos = np.cos(freqs).astype(np.float32)
    sin = np.sin(freqs).astype(np.float32)
    cos_tok = np.ascontiguousarray(
        cos.reshape(NTB, 128, 64).transpose(1, 0, 2).reshape(128, NTB * 64))
    sin_tok = np.ascontiguousarray(
        sin.reshape(NTB, 128, 64).transpose(1, 0, 2).reshape(128, NTB * 64))

    masks = np.zeros((128, 4 * 512), np.float32)
    qf = np.arange(512)
    for r in range(4):
        for kp in range(128):
            masks[kp, r * 512:(r + 1) * 512] = np.where(r * 128 + kp > qf, NEG, 0.0)

    ones = np.ones((128, 128), np.float32)
    ident = np.eye(128, dtype=np.float32)

    gates = np.ascontiguousarray(
        gates.reshape(NEXP, NHC, 128, NFC, 128).transpose(0, 3, 2, 1, 4)
        .reshape(NEXP, NFC, 128, NHC * 128))
    ups = np.ascontiguousarray(
        ups.reshape(NEXP, NHC, 128, NFC, 128).transpose(0, 3, 2, 1, 4)
        .reshape(NEXP, NFC, 128, NHC * 128))
    downs = np.ascontiguousarray(
        downs.reshape(NEXP, NFC, 128, 4, 512).transpose(0, 3, 1, 2, 4))
    common = dict(gwt=gwt, gates=gates, ups=ups, downs=downs,
                  cos_tok=cos_tok, sin_tok=sin_tok, masks=masks,
                  ones_in=ones, ident_in=ident, x_full=x)
    in_maps = []
    for c in range(NCORES):
        hA = 2 * c
        wq = wqkv[:, hA * 128:hA * 128 + 256]
        wk = wqkv[:, D + hA * 128:D + hA * 128 + 256]
        wv = wqkv[:, 2 * D + hA * 128:2 * D + hA * 128 + 256]
        m = dict(common)
        wq3 = np.concatenate([wq, wk, wv], axis=1)
        m["wqkv"] = np.ascontiguousarray(
            wq3.reshape(NHC, 128, 768).transpose(1, 0, 2).reshape(128, NHC * 768))
        wo_s = wo[hA * 128:hA * 128 + 256, :]
        m["wo"] = np.ascontiguousarray(
            wo_s.reshape(2, 128, D).transpose(1, 0, 2).reshape(128, 2 * D))
        m["x_own"] = np.ascontiguousarray(x[c * TOWN:(c + 1) * TOWN, :])
        in_maps.append(m)
    return in_maps


_NC_CACHE = [None]
_BUILD_PHASES = 99
_ATTN_CHUNKS = 9999



def kernel(**inputs):
    in_maps = _host_inputs(**inputs)
    if _NC_CACHE[0] is None:
        _NC_CACHE[0] = build_nc()
    nc = _NC_CACHE[0]
    res = run_bass_kernel_spmd(nc, in_maps, core_ids=list(range(NCORES)))
    out = np.concatenate([np.asarray(res.results[c]["y"]) for c in range(NCORES)],
                         axis=0)
    return out.reshape(1, T, D)
